# version 01
# baseline (speedup 1.0000x reference)
import contextlib
import math
import os

import numpy as np
import ml_dtypes

import concourse.bass as bass
import concourse.mybir as mybir
from concourse.bass_utils import run_bass_kernel_spmd

F32 = mybir.dt.float32
BF16 = mybir.dt.bfloat16
AF = mybir.ActivationFunctionType
ALU = mybir.AluOpType
AX = mybir.AxisListType.X

NCORES = 8
TOK = 1024
D = 2048
DK = 16
E2 = 4096
EC = 32
NH = 16
NKV = 4

MAGIC = float(np.float32(12582912.0))
LN_EPS = 1e-5
RMS_EPS = 1e-6
Q_EPS = 1e-5
ALPHA1 = float(1.0 / math.sqrt(2048.0))
ALPHA2 = float(1.0 / math.sqrt(4096.0))
SM_SCALE = float(1.0 / math.sqrt(128.0))
SM_SHIFT = -20.0

BF = ml_dtypes.bfloat16


class Em:
    ENGINES = ("pe", "act", "dve", "pool", "sp")
    QUEUES = {"Dsp": "sp", "Dact": "act", "Dpool": "pool"}

    def __init__(self):
        self.ops = []
        self.last_w = {}
        self.readers = {}
        self.bar = set()

    PHYS = ("PSA", "PSB", "PSC", "PSR0", "PSR1")

    @classmethod
    def _expand(cls, names):
        out = list(names)
        for b in names:
            for p in cls.PHYS:
                if b.startswith(p):
                    out.append(p)
                    break
        return out

    def _rec(self, eng, sem, step, fn, reads, writes):
        reads = self._expand(reads)
        writes = self._expand(writes)
        deps = set(self.bar)
        for b in reads:
            if b in self.last_w:
                deps.add(self.last_w[b])
        for b in writes:
            if b in self.last_w:
                deps.add(self.last_w[b])
            for r in self.readers.get(b, ()):
                deps.add(r)
        i = len(self.ops)
        deps.discard(i)
        self.ops.append(dict(eng=eng, sem=sem, step=step, fn=fn, deps=deps))
        for b in reads:
            self.readers.setdefault(b, []).append(i)
        for b in writes:
            self.last_w[b] = i
            self.readers[b] = [i]
        return i

    def c(self, eng, reads, writes, fn):
        return self._rec(eng, eng, 1, fn, reads, writes)

    def dma(self, q, reads, writes, fn):
        return self._rec(self.QUEUES[q], q, 16, fn, reads, writes)

    def cc(self, reads, writes, fn):
        return self._rec("pool", "cc", 1, fn, reads, writes)

    def barrier(self):
        keep = {}
        for i, o in enumerate(self.ops):
            keep[(o["eng"], o["sem"])] = i
        self.bar = set(keep.values())

    def finalize(self):
        needed = set()
        for o in self.ops:
            needed |= o["deps"]
        lastsem = {}
        for i, o in enumerate(self.ops):
            lastsem[o["sem"]] = i
        needed |= set(lastsem.values())
        semval = {}
        for i, o in enumerate(self.ops):
            if o["sem"] in self.QUEUES or o["sem"] == "cc" or i in needed:
                semval[o["sem"]] = semval.get(o["sem"], 0) + o["step"]
                o["inc"], o["val"] = True, semval[o["sem"]]
            else:
                o["inc"], o["val"] = False, None
        for o in self.ops:
            w = {}
            for d in o["deps"]:
                p = self.ops[d]
                if p["val"] is None:
                    raise RuntimeError("dep target not marked")
                w[p["sem"]] = max(w.get(p["sem"], 0), p["val"])
            o["waits"] = w
        self.final_vals = dict(semval)
        qs = {e: [] for e in self.ENGINES}
        for i, o in enumerate(self.ops):
            qs[o["eng"]].append(i)
        pos = {e: 0 for e in self.ENGINES}
        cur = {}
        prog, ndone = True, 0
        while prog:
            prog = False
            for e in self.ENGINES:
                while pos[e] < len(qs[e]):
                    o = self.ops[qs[e][pos[e]]]
                    if all(cur.get(s, 0) >= v for s, v in o["waits"].items()):
                        if o["inc"]:
                            cur[o["sem"]] = cur.get(o["sem"], 0) + o["step"]
                        pos[e] += 1
                        ndone += 1
                        prog = True
                    else:
                        break
        if ndone != len(self.ops):
            stuck = {e: pos[e] for e in self.ENGINES if pos[e] < len(qs[e])}
            raise RuntimeError(f"schedule deadlock, stuck={stuck}, done {ndone}/{len(self.ops)}")

    def replay(self, eng, h, sems):
        for o in self.ops:
            if o["eng"] != eng:
                continue
            for s, v in sorted(o["waits"].items()):
                h.wait_ge(sems[s], v)
            inst = o["fn"](h)
            if o["inc"]:
                inst.then_inc(sems[o["sem"]], o["step"])


def pack_kc(w):
    K, M = w.shape
    return np.ascontiguousarray(
        np.transpose(w.reshape(K // 128, 128, M), (1, 0, 2)).reshape(128, -1))


def pack_mc(w, mt=128):
    K, M = w.shape
    a = w.reshape(K // 128, 128, M // mt, mt)
    return np.ascontiguousarray(np.transpose(a, (1, 2, 0, 3)).reshape(128, -1))


def hadamard128():
    i = np.arange(128)
    a = i[:, None] & i[None, :]
    pc = np.zeros((128, 128), np.int64)
    v = a.copy()
    while v.any():
        pc += v & 1
        v >>= 1
    return np.where(pc % 2 == 0, 1.0, -1.0).astype(np.float32)


def build_program(n_sub=4):
    nc = bass.Bass("TRN2", target_bir_lowering=False, debug=False)

    x_in = nc.dram_tensor("x_fm", [D, TOK], F32, kind="ExternalInput").ap()
    wq1_i = nc.dram_tensor("wq1", [128, D], F32, kind="ExternalInput").ap()
    wk1_i = nc.dram_tensor("wk1", [128, D], F32, kind="ExternalInput").ap()
    aw1_i = nc.dram_tensor("aw1", [128, EC * 2048], F32, kind="ExternalInput").ap()
    aw2_i = nc.dram_tensor("aw2", [128, EC * 256], F32, kind="ExternalInput").ap()
    wo_i = nc.dram_tensor("wo_a", [128, DK * E2], F32, kind="ExternalInput").ap()
    nw_i = nc.dram_tensor("nw_fm", [128, EC], F32, kind="ExternalInput").ap()
    wqup_sl = nc.dram_tensor("wq_up_sl", [256, E2], F32, kind="ExternalInput").ap()
    wqdn_sl = nc.dram_tensor("wq_dn_sl", [512, D], F32, kind="ExternalInput").ap()
    bwq_i = nc.dram_tensor("bwq", [128, NH * D], F32, kind="ExternalInput").ap()
    bwk_i = nc.dram_tensor("bwk", [128, NKV * D], F32, kind="ExternalInput").ap()
    bwv_i = nc.dram_tensor("bwv", [128, NKV * D], F32, kind="ExternalInput").ap()
    bwo_i = nc.dram_tensor("bwo", [128, DK * D], F32, kind="ExternalInput").ap()
    hpm_i = nc.dram_tensor("hpm", [128, 256], BF16, kind="ExternalInput").ap()
    trim_i = nc.dram_tensor("trim", [128, 256], BF16, kind="ExternalInput").ap()
    psw_i = nc.dram_tensor("psw", [128, 128], BF16, kind="ExternalInput").ap()
    idf_i = nc.dram_tensor("idf", [128, 128], F32, kind="ExternalInput").ap()
    rope_i = nc.dram_tensor("ropetab", [128, 2 * TOK], F32, kind="ExternalInput").ap()
    fmask_i = nc.dram_tensor("fmask", [256, TOK], BF16, kind="ExternalInput").ap()
    csel_i = nc.dram_tensor("csel", [128, 24], F32, kind="ExternalInput").ap()

    y_out = nc.dram_tensor("y_fm", [D, TOK], F32, kind="ExternalOutput").ap()
    dbg_out = nc.dram_tensor("dbg", [128, 96 + 4 * TOK], F32, kind="ExternalOutput").ap() if os.environ.get("KDBG") else None

    ar_in = nc.dram_tensor("ar_in", [1, 2], F32)
    ar_out = nc.dram_tensor("ar_out", [1, 2], F32)
    agu_in = nc.dram_tensor("agu_in", [256, E2], BF16)
    agu_out = nc.dram_tensor("agu_out", [D, E2], BF16)
    agd_in = nc.dram_tensor("agd_in", [512, D], BF16)
    agd_out = nc.dram_tensor("agd_out", [E2, D], BF16)
    car_in = nc.dram_tensor("car_in", [128, 64], F32)
    car_out = nc.dram_tensor("car_out", [NCORES * 128, 64], F32)
    kv_in = nc.dram_tensor("kv_in", [128, 6144], BF16)
    kv_out = nc.dram_tensor("kv_out", [NCORES * 128, 6144], BF16)

    em = Em()
    ctx = contextlib.ExitStack()
    with ctx:
        sb = lambda n, s, d: ctx.enter_context(nc.sbuf_tensor(n, s, d))

        XRES = sb("XRES", [128, DK * TOK], F32)
        SA = sb("SA", [128, EC * TOK], BF16)
        HPM = sb("HPM", [128, 256], BF16)
        TRIM = sb("TRIM", [128, 256], BF16)
        PSW = sb("PSW", [128, 128], BF16)
        IDF = sb("IDF", [128, 128], F32)
        NW = sb("NW", [128, EC], F32)
        CSEL = sb("CSEL", [128, 24], F32)
        ONE_F = sb("ONE_F", [128, 1], F32)
        ONE_B = sb("ONE_B", [128, 1], BF16)
        ONE_R = sb("ONE_R", [1, 128], F32)
        CPOOL = sb("CPOOL", [128, 8], F32)
        ROWS = sb("ROWS", [1, 1536], F32)
        SROW = sb("SROW", [1, 16], F32)
        SWB = sb("SWB", [128, 4], F32)

        PSA = ctx.enter_context(nc.psum_tensor("PSA", [128, TOK], F32))
        PSB = ctx.enter_context(nc.psum_tensor("PSB", [128, TOK], F32))
        PSC = ctx.enter_context(nc.psum_tensor("PSC", [128, TOK], F32))
        PSR0 = ctx.enter_context(nc.psum_tensor("PSR0", [1, 512], F32))
        PSR1 = ctx.enter_context(nc.psum_tensor("PSR1", [1, 512], F32))

        CONSTS = [LN_EPS, RMS_EPS, SM_SHIFT, -0.5, float(1.0 / E2), SM_SCALE]
        for k, v in enumerate(CONSTS):
            nc.const_aps.register(F32, float(v), CPOOL[:, k : k + 1])

        def memset(ap, val, name):
            em.c("dve", [], [name], lambda e, a=ap, v=val: e.memset(a, v))

        def dmain(q, dst, src, wname, reads=()):
            em.dma(q, list(reads), [wname], lambda e, d=dst, s=src: e.dma_start(out=d, in_=s))

        memset(ONE_F[:, :], 1.0, "ONE_F")
        memset(ONE_B[:, :], 1.0, "ONE_B")
        memset(ONE_R[:, :], 1.0, "ONE_R")
        for k, v in enumerate(CONSTS):
            em.c("dve", [], ["CPOOL"], lambda e, k=k, v=v: e.memset(CPOOL[:, k : k + 1], float(v)))
        dmain("Dsp", XRES[:, :].rearrange("p (c t) -> p c t", c=DK), x_in.rearrange("(c p) t -> p c t", p=128), "XRES_all")
        for kc in range(DK):
            em.c("dve", ["XRES_all"], [f"XRES{kc}"], lambda e: e.engine_nop())
        dmain("Dsp", HPM[:, :], hpm_i[:, :], "HPM")
        dmain("Dsp", TRIM[:, :], trim_i[:, :], "TRIM")
        dmain("Dsp", PSW[:, :], psw_i[:, :], "PSW")
        dmain("Dsp", IDF[:, :], idf_i[:, :], "IDF")
        dmain("Dsp", NW[:, :], nw_i[:, :], "NW")
        dmain("Dsp", CSEL[:, :], csel_i[:, :], "CSEL")

        QMODE = int(os.environ.get("KQM", "3"))
        with contextlib.ExitStack() as qsx:
            qsb = lambda n, s, d: qsx.enter_context(nc.sbuf_tensor(n, s, d))
            W2 = [qsb(f"W2_{i}", [128, D], F32) for i in range(2)]
            QOUT = qsb("QOUT", [128, D], BF16)
            QT = qsb("QT", [128, D], F32)
            QSUM = qsb("QSUM", [128, 12], F32)

            halves = ([("u", i, hf, wqup_sl[128 * i : 128 * (i + 1), 2048 * hf : 2048 * (hf + 1)],
                        agu_in.ap()[128 * i : 128 * (i + 1), 2048 * hf : 2048 * (hf + 1)])
                       for i in range(2) for hf in range(2)]
                      + [("d", j, 0, wqdn_sl[128 * j : 128 * (j + 1), :],
                          agd_in.ap()[128 * j : 128 * (j + 1), :]) for j in range(4)])
            halves = list(halves)
            for t, (kind, i, hf, srcap, dstap) in enumerate(halves):
                sl = t % 2
                dmain("Dsp", W2[sl][:, :], srcap, f"W2_{sl}")
                em.c("dve", [f"W2_{sl}"], [f"QS{t}"],
                     lambda e, t=t, sl=sl: e.reduce_sum(QSUM[:, t : t + 1], W2[sl][:, :], AX,
                                                        apply_absolute_value=True))
            em.c("pe", [f"QS{t}" for t in range(8)] + ["ONE_F"], ["PSR0_q"],
                 lambda e: e.matmul(PSR0[0:1, 0:12], ONE_F[:, :], QSUM[:, 0:12], start=True, stop=True))
            em.c("dve", ["PSR0_q"], ["SROWq"],
                 lambda e: e.tensor_copy(SROW[0:1, 0:12], PSR0[0:1, 0:12]))
            em.c("dve", ["SROWq"], ["SROWu0"],
                 lambda e: e.tensor_add(SROW[0:1, 12:14], SROW[0:1, 0:2], SROW[0:1, 2:4]))
            em.c("dve", ["SROWu0"], ["SROWu"],
                 lambda e: e.tensor_add(SROW[0:1, 8:9], SROW[0:1, 12:13], SROW[0:1, 13:14]))
            em.c("dve", ["SROWq"], ["SROWd0"],
                 lambda e: e.tensor_add(SROW[0:1, 12:14], SROW[0:1, 4:6], SROW[0:1, 6:8]))
            em.c("dve", ["SROWd0"], ["SROWd"],
                 lambda e: e.tensor_add(SROW[0:1, 9:10], SROW[0:1, 12:13], SROW[0:1, 13:14]))
            em.dma("Dsp", ["SROWu", "SROWd"], ["ar_in"],
                   lambda e: e.dma_start(out=ar_in.ap()[0:1, 0:2], in_=SROW[0:1, 8:10]))
            if QMODE >= 2:
                em.cc(["ar_in"], ["ar_out"],
                      lambda e: e.collective_compute("AllReduce", ALU.add,
                                                     replica_groups=[list(range(NCORES))],
                                                     ins=[ar_in.ap().opt()], outs=[ar_out.ap().opt()]))
                dmain("Dsp", SROW[0:1, 10:12], ar_out.ap()[0:1, 0:2], "SROW_AR", reads=["ar_out"])
            else:
                em.c("dve", ["SROWu", "SROWd"], ["SROW_AR"],
                     lambda e: e.tensor_copy(SROW[0:1, 10:12], SROW[0:1, 8:10]))
            em.c("dve", ["SROW_AR"], ["SROW_SW"],
                 lambda e: e.tensor_scalar(SROW[0:1, 12:14], SROW[0:1, 10:12],
                                           float(1.0 / (E2 * D)), float(Q_EPS), ALU.mult, ALU.max))
            em.c("dve", ["SROW_SW"], ["SROW_SI"],
                 lambda e: e.reciprocal(SROW[0:1, 14:16], SROW[0:1, 12:14]))
            em.c("pe", ["SROW_SI", "SROW_SW", "ONE_R"], ["PSC_sw"],
                 lambda e: e.matmul(PSC[:, 0:4], ONE_R[0:1, :], SROW[0:1, 12:16], start=True, stop=True))
            em.c("dve", ["PSC_sw"], ["SWB"], lambda e: e.tensor_copy(SWB[:, :], PSC[:, 0:4]))

            for t, (kind, i, hf, srcap, dstap) in enumerate(halves):
                sl = t % 2
                si = 2 if kind == "u" else 3
                dmain("Dsp", W2[sl][:, :], srcap, f"W2_{sl}")
                em.c("pool", [f"W2_{sl}", "SWB"], ["QTx", "QTB"],
                     lambda e, sl=sl, si=si: e.tensor_scalar_mul(QT[:, :], W2[sl][:, :], SWB[:, si : si + 1]))
                em.c("dve", ["QTx"], ["QTy", "QTB"],
                     lambda e: e.tensor_scalar_add(QT[:, :], QT[:, :], MAGIC))
                em.c("dve", ["QTy"], ["QTz", "QTB"],
                     lambda e: e.tensor_scalar(QT[:, :], QT[:, :], -MAGIC, 1.0, ALU.add, ALU.min))
                em.c("dve", ["QTz", "QTB"], ["QOx"],
                     lambda e: e.tensor_scalar_max(QOUT[:, :], QT[:, :], -1.0))
                em.dma("Dsp", ["QOx"], [f"agq{t}"],
                       lambda e, dstap=dstap: e.dma_start(out=dstap, in_=QOUT[:, :]))
            if QMODE in (3, 4):
                em.cc([f"agq{t}" for t in range(8)], ["agu_out"],
                      lambda e: e.collective_compute("AllGather", ALU.bypass,
                                                     replica_groups=[list(range(NCORES))],
                                                     ins=[agu_in.ap().opt()], outs=[agu_out.ap().opt()]))
            if QMODE in (3, 5):
                em.cc([f"agq{t}" for t in range(8)], ["agd_out"],
                      lambda e: e.collective_compute("AllGather", ALU.bypass,
                                                     replica_groups=[list(range(NCORES))],
                                                     ins=[agd_in.ap().opt()], outs=[agd_out.ap().opt()]))
            if QMODE < 3:
                em.c("dve", [f"agq{t}" for t in range(8)], ["agu_out", "agd_out"],
                     lambda e: e.engine_nop())
        em.barrier()

        def bcast_row(row_name, row_ap, ncols, ps, psname, dst_name, dst_ap):
            for o in range(0, ncols, 512):
                w = min(512, ncols - o)
                em.c("pe", [row_name, "ONE_R"], [psname],
                     lambda e, o=o, w=w: e.matmul(ps[:, o : o + w], ONE_R[0:1, :],
                                                  row_ap[0:1, o : o + w], start=True, stop=True))
            em.c("dve", [psname], [dst_name],
                 lambda e: e.tensor_copy(dst_ap[:, 0:ncols], ps[:, 0:ncols]))

        def emit_ln(tag, XN, xn_pref):
            with contextlib.ExitStack() as ls:
                lsb = lambda n, s, d: ls.enter_context(nc.sbuf_tensor(n, s, d))
                X2 = [lsb(f"{tag}X2_{i}", [128, 512], F32) for i in range(2)]
                ABC = lsb(f"{tag}ABC", [128, 1024], F32)
                TT = [lsb(f"{tag}TT{i}", [128, 512], F32) for i in range(2)]
                R = ROWS
                for h in range(2):
                    for kc in range(DK):
                        xa = XRES[:, kc * TOK + 512 * h : kc * TOK + 512 * (h + 1)]
                        sl = kc % 2
                        em.c("act", [f"XRES{kc}"], [f"{tag}x2_{sl}"],
                             lambda e, xa=xa, sl=sl: e.activation(X2[sl][:, :], xa, AF.Square))
                        em.c("pe", [f"XRES{kc}", "ONE_F"], [f"PSR0_{tag}{h}"],
                             lambda e, xa=xa, kc=kc: e.matmul(PSR0[:, :], ONE_F[:, :], xa,
                                                              start=(kc == 0), stop=(kc == DK - 1)))
                        em.c("pe", [f"{tag}x2_{sl}", "ONE_F"], [f"PSR1_{tag}{h}"],
                             lambda e, sl=sl, kc=kc: e.matmul(PSR1[:, :], ONE_F[:, :], X2[sl][:, :],
                                                              start=(kc == 0), stop=(kc == DK - 1)))
                    em.c("dve", [f"PSR0_{tag}{h}", "ROWSBUF"], [f"{tag}m{h}", "ROWSBUF"],
                         lambda e: e.tensor_scalar_mul(R[0:1, 0:512], PSR0[0:1, :], float(1.0 / D)))
                    em.c("dve", [f"PSR1_{tag}{h}", "ROWSBUF"], [f"{tag}q{h}", "ROWSBUF"],
                         lambda e: e.tensor_scalar_mul(R[0:1, 512:1024], PSR1[0:1, :], float(1.0 / D)))
                    em.c("dve", [f"{tag}m{h}"], [f"{tag}mm{h}", "ROWSBUF"],
                         lambda e: e.tensor_mul(R[0:1, 1024:1536], R[0:1, 0:512], R[0:1, 0:512]))
                    em.c("dve", [f"{tag}q{h}", f"{tag}mm{h}"], [f"{tag}v{h}", "ROWSBUF"],
                         lambda e: e.tensor_sub(R[0:1, 512:1024], R[0:1, 512:1024], R[0:1, 1024:1536]))
                    em.c("act", [f"{tag}v{h}", "CPOOL"], [f"{tag}l{h}", "ROWSBUF"],
                         lambda e: e.activation(R[0:1, 512:1024], R[0:1, 512:1024], AF.Ln, bias=LN_EPS))
                    em.c("act", [f"{tag}l{h}", "CPOOL"], [f"{tag}r{h}", "ROWSBUF"],
                         lambda e: e.activation(R[0:1, 512:1024], R[0:1, 512:1024], AF.Exp, scale=-0.5))
                    em.c("dve", [f"{tag}r{h}", f"{tag}m{h}"], [f"{tag}b{h}", "ROWSBUF"],
                         lambda e: e.scalar_tensor_tensor(R[0:1, 1024:1536], R[0:1, 0:512], -1.0,
                                                          R[0:1, 512:1024], ALU.mult, ALU.mult))
                    em.c("pe", [f"{tag}r{h}", "ONE_R"], [f"PSC_{tag}a{h}"],
                         lambda e: e.matmul(PSC[:, 0:512], ONE_R[0:1, :], R[0:1, 512:1024],
                                            start=True, stop=True))
                    em.c("dve", [f"PSC_{tag}a{h}"], [f"{tag}abca{h}"],
                         lambda e: e.tensor_copy(ABC[:, 0:512], PSC[:, 0:512]))
                    em.c("pe", [f"{tag}b{h}", "ONE_R"], [f"PSC_{tag}b{h}"],
                         lambda e: e.matmul(PSC[:, 512:1024], ONE_R[0:1, :], R[0:1, 1024:1536],
                                            start=True, stop=True))
                    em.c("dve", [f"PSC_{tag}b{h}"], [f"{tag}abcb{h}"],
                         lambda e: e.tensor_copy(ABC[:, 512:1024], PSC[:, 512:1024]))
                    for kc in range(DK):
                        xa = XRES[:, kc * TOK + 512 * h : kc * TOK + 512 * (h + 1)]
                        xna = XN[:, kc * TOK + 512 * h : kc * TOK + 512 * (h + 1)]
                        sl = kc % 2
                        em.c("pool", [f"XRES{kc}", f"{tag}abca{h}"], [f"{tag}tt{sl}"],
                             lambda e, xa=xa, sl=sl: e.tensor_mul(TT[sl][:, :], xa, ABC[:, 0:512]))
                        em.c("dve", [f"{tag}tt{sl}", f"{tag}abcb{h}"], [f"{xn_pref}{kc}"],
                             lambda e, xna=xna, sl=sl: e.tensor_add(xna, TT[sl][:, :], ABC[:, 512:1024]))

        def col_to_row(cname, col_ap, rname, row_ap):
            em.c("pe", [cname, "IDF"], ["PSC_" + rname],
                 lambda e: e.transpose(PSC[0:1, 0:128], col_ap, IDF[:, :]))
            em.c("dve", ["PSC_" + rname], [rname],
                 lambda e: e.tensor_copy(row_ap, PSC[0:1, 0:128]))

        def emit_attn_a(XN, xnp):
            with contextlib.ExitStack() as s:
                ssb = lambda n, sh, d: s.enter_context(nc.sbuf_tensor(n, sh, d))
                Q1 = ssb("Q1", [128, TOK], BF16)
                F1 = ssb("F1", [128, TOK], BF16)
                with contextlib.ExitStack() as ws:
                    WQ1 = ws.enter_context(nc.sbuf_tensor("WQ1", [128, D], BF16))
                    WK1 = ws.enter_context(nc.sbuf_tensor("WK1", [128, D], BF16))
                    dmain("Dpool", WQ1[:, :], wq1_i[:, :], "WQ1")
                    dmain("Dpool", WK1[:, :], wk1_i[:, :], "WK1")
                    for h in range(2):
                        cols = slice(512 * h, 512 * (h + 1))
                        for kc in range(DK):
                            xna = XN[:, kc * TOK + 512 * h : kc * TOK + 512 * (h + 1)]
                            em.c("pe", [f"{xnp}{kc}", "WQ1"], ["PSA_q1"],
                                 lambda e, xna=xna, kc=kc, cols=cols: e.matmul(
                                     PSA[:, cols], WQ1[:, kc * 128 : (kc + 1) * 128], xna,
                                     start=(kc == 0), stop=(kc == DK - 1)))
                            em.c("pe", [f"{xnp}{kc}", "WK1"], ["PSB_f1"],
                                 lambda e, xna=xna, kc=kc, cols=cols: e.matmul(
                                     PSB[:, cols], WK1[:, kc * 128 : (kc + 1) * 128], xna,
                                     start=(kc == 0), stop=(kc == DK - 1)))
                    em.c("act", ["PSA_q1"], ["Q1"], lambda e: e.activation(Q1[:, :], PSA[:, :], AF.Identity))
                    em.c("act", ["PSB_f1"], ["F1"], lambda e: e.activation(F1[:, :], PSB[:, :], AF.Identity))
                em.barrier()

                CAR = ssb("CAR", [128, 64], F32)
                AGT = ssb("AGT", [128, NCORES * 64], F32)
                SCARRY = ssb("SCARRY", [128, 32], F32)
                AW = [ssb(f"AW{i}", [128, 1024], BF16) for i in range(2)]
                AW2 = [ssb(f"AW2_{i}", [128, 256], BF16) for i in range(2)]
                FT = [ssb("FT0", [128, TOK], F32)]
                ATL = ssb("ATL", [128, TOK], F32)
                HT = [ssb("HT0", [128, TOK], F32)]
                QC = ssb("QC", [128, TOK], BF16)
                RBC = ssb("RBC", [128, TOK], F32)
                WOB = [ssb(f"WOB{i}", [128, D], BF16) for i in range(2)]
                APR = ssb("APR", [128, 32], F32)
                HPR = ssb("HPR", [128, 32], F32)

                for c in range(EC):
                    sl = c % 2
                    ps, psn = (PSA, "PSA") if sl == 0 else (PSB, "PSB")
                    ps2, ps2n = (PSB, "PSB") if sl == 0 else (PSA, "PSA")
                    dmain("Dpool", AW2[sl][:, :], aw2_i[:, 256 * c : 256 * (c + 1)], f"AW2_{sl}")
                    for h in range(2):
                        cols = slice(512 * h, 512 * (h + 1))
                        em.c("pe", ["F1", f"AW2_{sl}"], [f"{psn}_f{c}"],
                             lambda e, ps=ps, cols=cols, sl=sl: e.matmul(
                                 ps[:, cols], AW2[sl][:, 128:256], F1[:, cols], start=True, stop=True))
                    em.c("act", [f"{psn}_f{c}"], ["ft"],
                         lambda e, ps=ps, sl=sl: e.activation(FT[0][:, :], ps[:, :], AF.Sigmoid))
                    em.c("act", ["ft"], ["atl"],
                         lambda e, sl=sl: e.activation(ATL[:, :], FT[0][:, :], AF.Sigmoid))
                    for g in range(2):
                        aws = (2 * c + g) % 2
                        dmain("Dpool", AW[aws][:, :],
                              aw1_i[:, 2048 * c + 1024 * g : 2048 * c + 1024 * (g + 1)], f"AW{aws}")
                        for h in range(2):
                            cols = slice(512 * h, 512 * (h + 1))
                            for kk in range(8):
                                kc = 8 * g + kk
                                xna = XN[:, kc * TOK + 512 * h : kc * TOK + 512 * (h + 1)]
                                em.c("pe", [f"{xnp}{kc}", f"AW{aws}"], [f"{ps2n}_v{c}"],
                                     lambda e, ps2=ps2, cols=cols, aws=aws, kk=kk, kc=kc, xna=xna: e.matmul(
                                         ps2[:, cols], AW[aws][:, kk * 128 : (kk + 1) * 128], xna,
                                         start=(kc == 0), stop=(kc == DK - 1)))
                    em.c("dve", ["atl", f"{ps2n}_v{c}"], [f"SAb{c}"],
                         lambda e, ps2=ps2, c=c: e.scalar_tensor_tensor(
                             SA[:, TOK * c : TOK * (c + 1)], ATL[:, :], 1.0, ps2[:, :],
                             ALU.subtract, ALU.mult))
                    em.c("dve", ["atl", f"SAb{c}"], ["ht"],
                         lambda e, sl=sl, c=c: e.tensor_tensor_scan(
                             HT[0][:, :], ATL[:, :], SA[:, TOK * c : TOK * (c + 1)],
                             0.0, ALU.mult, ALU.add))
                    em.c("dve", ["ht"], [f"carh{c}"],
                         lambda e, sl=sl, c=c: e.tensor_copy(CAR[:, c : c + 1], HT[0][:, TOK - 1 : TOK]))
                    em.c("dve", ["atl"], [f"cara{c}"],
                         lambda e, c=c: e.tensor_reduce(CAR[:, 32 + c : 33 + c], ATL[:, :], AX, ALU.mult))

                em.dma("Dsp", [f"carh{c}" for c in range(EC)] + [f"cara{c}" for c in range(EC)],
                       ["car_in"], lambda e: e.dma_start(out=car_in.ap()[:, :], in_=CAR[:, :]))
                em.cc(["car_in"], ["car_out"],
                      lambda e: e.collective_compute("AllGather", ALU.bypass,
                                                     replica_groups=[list(range(NCORES))],
                                                     ins=[car_in.ap().opt()], outs=[car_out.ap().opt()]))
                dmain("Dsp", AGT[:, :].rearrange("p (r f) -> p r f", r=NCORES),
                      car_out.ap().rearrange("(r p) f -> p r f", p=128), "AGT", reads=["car_out"])
                memset(SCARRY[:, :], 0.0, "SCARRY")
                for r in range(NCORES):
                    em.c("dve", ["AGT", "CSEL"], ["APR"],
                         lambda e, r=r: e.tensor_scalar_mul(APR[:, :], AGT[:, 64 * r + 32 : 64 * r + 64],
                                                            CSEL[:, 8 + r : 9 + r]))
                    em.c("dve", ["APR", "CSEL"], ["APR"],
                         lambda e, r=r: e.tensor_scalar(APR[:, :], APR[:, :], 1.0,
                                                        CSEL[:, 8 + r : 9 + r], ALU.add, ALU.subtract))
                    em.c("dve", ["AGT", "CSEL"], ["HPR"],
                         lambda e, r=r: e.tensor_scalar_mul(HPR[:, :], AGT[:, 64 * r : 64 * r + 32],
                                                            CSEL[:, 8 + r : 9 + r]))
                    em.c("pool", ["SCARRY", "APR"], ["SCARRY"],
                         lambda e: e.tensor_mul(SCARRY[:, :], SCARRY[:, :], APR[:, :]))
                    em.c("dve", ["SCARRY", "HPR"], ["SCARRY"],
                         lambda e: e.tensor_add(SCARRY[:, :], SCARRY[:, :], HPR[:, :]))

                if dbg_out is not None:
                    em.dma("Dsp", [f"carh{c}" for c in range(EC)] + [f"cara{c}" for c in range(EC)],
                           ["dbg1"], lambda e: e.dma_start(out=dbg_out[:, 0:64], in_=CAR[:, :]))
                    em.dma("Dsp", ["SCARRY"], ["dbg2"],
                           lambda e: e.dma_start(out=dbg_out[:, 64:96], in_=SCARRY[:, :]))
                    em.dma("Dsp", ["atl"], ["dbg3"],
                           lambda e: e.dma_start(out=dbg_out[:, 96 : 96 + TOK], in_=ATL[:, :]))
                    em.dma("Dsp", ["ht"], ["dbg4"],
                           lambda e: e.dma_start(out=dbg_out[:, 96 + TOK : 96 + 2 * TOK], in_=HT[0][:, :]))
                    em.c("dve", ["SAb31", "RBC"], ["dbg5c", "RBC"],
                         lambda e: e.tensor_copy(RBC[:, :], SA[:, TOK * 31 : TOK * 32]))
                    em.dma("Dsp", ["dbg5c"], ["dbg5"],
                           lambda e: e.dma_start(out=dbg_out[:, 96 + 2 * TOK : 96 + 3 * TOK], in_=RBC[:, :]))
                for c in range(EC):
                    sl = c % 2
                    ps, psn = (PSA, "PSA") if sl == 0 else (PSB, "PSB")
                    ps2, ps2n = (PSB, "PSB") if sl == 0 else (PSA, "PSA")
                    dmain("Dpool", AW2[sl][:, :], aw2_i[:, 256 * c : 256 * (c + 1)], f"AW2_{sl}")
                    for h in range(2):
                        cols = slice(512 * h, 512 * (h + 1))
                        em.c("pe", ["F1", f"AW2_{sl}"], [f"{psn}_f2{c}"],
                             lambda e, ps=ps, cols=cols, sl=sl: e.matmul(
                                 ps[:, cols], AW2[sl][:, 128:256], F1[:, cols], start=True, stop=True))
                        em.c("pe", ["Q1", f"AW2_{sl}"], [f"{ps2n}_q{c}"],
                             lambda e, ps2=ps2, cols=cols, sl=sl: e.matmul(
                                 ps2[:, cols], AW2[sl][:, 0:128], Q1[:, cols], start=True, stop=True))
                    em.c("act", [f"{psn}_f2{c}"], ["ft"],
                         lambda e, ps=ps, sl=sl: e.activation(FT[0][:, :], ps[:, :], AF.Sigmoid))
                    em.c("act", ["ft"], ["atl"],
                         lambda e, sl=sl: e.activation(ATL[:, :], FT[0][:, :], AF.Sigmoid))
                    em.c("act", [f"{ps2n}_q{c}"], ["qc"],
                         lambda e, ps2=ps2: e.activation(QC[:, :], ps2[:, :], AF.Silu))
                    em.c("dve", ["atl", f"SAb{c}", "SCARRY"], ["ht"],
                         lambda e, sl=sl, c=c: e.tensor_tensor_scan(
                             HT[0][:, :], ATL[:, :], SA[:, TOK * c : TOK * (c + 1)],
                             SCARRY[:, c : c + 1], ALU.mult, ALU.add))
                    em.c("dve", ["ht", "qc"], ["ht"],
                         lambda e, sl=sl: e.scalar_tensor_tensor(HT[0][:, :], QC[:, :], -1.0,
                                                                 HT[0][:, :], ALU.mult, ALU.mult))
                    em.c("act", ["ht"], ["ft"],
                         lambda e, sl=sl: e.activation(FT[0][:, :], HT[0][:, :], AF.Square))
                    for h in range(2):
                        cols = slice(512 * h, 512 * (h + 1))
                        em.c("pe", ["ft", "ONE_F"], [f"PSR{h}_g2"],
                             lambda e, cols=cols, c=c, h=h, sl=sl: e.matmul(
                                 (PSR0 if h == 0 else PSR1)[:, :], ONE_F[:, :], FT[0][:, cols],
                                 start=(c == 0), stop=(c == EC - 1)))
                    em.c("dve", ["ht"], [f"SAg{c}"],
                         lambda e, sl=sl, c=c: e.tensor_copy(SA[:, TOK * c : TOK * (c + 1)], HT[0][:, :]))

                for h in range(2):
                    em.c("act", [f"PSR{h}_g2", "CPOOL", "ROWSBUF"], [f"rmsl{h}", "ROWSBUF"],
                         lambda e, h=h: e.activation(ROWS[0:1, 512 * h : 512 * (h + 1)],
                                                     (PSR0 if h == 0 else PSR1)[0:1, :],
                                                     AF.Ln, bias=RMS_EPS, scale=float(1.0 / E2)))
                    em.c("act", [f"rmsl{h}", "CPOOL"], [f"rmsr{h}", "ROWSBUF"],
                         lambda e, h=h: e.activation(ROWS[0:1, 512 * h : 512 * (h + 1)],
                                                     ROWS[0:1, 512 * h : 512 * (h + 1)],
                                                     AF.Exp, scale=-0.5))
                    em.c("pe", [f"rmsr{h}", "ONE_R"], [f"PSC_r{h}"],
                         lambda e, h=h: e.matmul(PSC[:, 512 * h : 512 * (h + 1)], ONE_R[0:1, :],
                                                 ROWS[0:1, 512 * h : 512 * (h + 1)],
                                                 start=True, stop=True))
                    em.c("dve", [f"PSC_r{h}"], ["RBC"],
                         lambda e, h=h: e.tensor_copy(RBC[:, 512 * h : 512 * (h + 1)],
                                                      PSC[:, 512 * h : 512 * (h + 1)]))
                for c in range(EC):
                    em.c("dve", [f"SAg{c}", "RBC", "NW"], [f"SAn{c}"],
                         lambda e, c=c: e.scalar_tensor_tensor(
                             SA[:, TOK * c : TOK * (c + 1)], SA[:, TOK * c : TOK * (c + 1)],
                             NW[:, c : c + 1], RBC[:, :], ALU.mult, ALU.mult))
                for m in range(DK):
                    for g in range(2):
                        sl = (2 * m + g) % 2
                        ps, psn = (PSA, "PSA") if m % 2 == 0 else (PSB, "PSB")
                        dmain("Dpool", WOB[sl][:, :],
                              wo_i[:, E2 * m + 2048 * g : E2 * m + 2048 * (g + 1)], f"WOB{sl}")
                        for h in range(2):
                            cols = slice(512 * h, 512 * (h + 1))
                            for kk in range(DK):
                                k = g * DK + kk
                                em.c("pe", [f"SAn{k}", f"WOB{sl}"], [f"{psn}_o{m}"],
                                     lambda e, ps=ps, sl=sl, k=k, kk=kk, h=h, cols=cols, g=g: e.matmul(
                                         ps[:, cols], WOB[sl][:, kk * 128 : (kk + 1) * 128],
                                         SA[:, TOK * k + 512 * h : TOK * k + 512 * (h + 1)],
                                         start=(k == 0), stop=(k == EC - 1)))
                    em.c("dve", [f"{psn}_o{m}", f"XRES{m}"], [f"XRES{m}"],
                         lambda e, ps=ps, m=m: e.tensor_add(XRES[:, TOK * m : TOK * (m + 1)],
                                                            XRES[:, TOK * m : TOK * (m + 1)], ps[:, :]))

        def emit_ffn(XN, xnp, tag):
            with contextlib.ExitStack() as s:
                ssb = lambda n, sh, d: s.enter_context(nc.sbuf_tensor(n, sh, d))
                STRIP = ssb(f"{tag}STRIP", [128, 4096], F32)
                STMP = ssb(f"{tag}STMP", [128, 256], F32)
                MXT = ssb(f"{tag}MXT", [128, 256], F32)
                MC = ssb(f"{tag}MC", [128, 8], F32)
                RC = ssb(f"{tag}RC", [128, 128], F32)
                RQ = ssb(f"{tag}RQ", [1, 256], F32)
                FBC = ssb(f"{tag}FBC", [128, 256], F32)
                DDB = ssb(f"{tag}DDB", [128, TOK], F32)
                HEVA = ssb(f"{tag}HEVA", [128, TOK], F32)
                WUP = [ssb(f"{tag}WUP{i}", [128, 1024], BF16) for i in range(2)]
                WDN = [ssb(f"{tag}WDN{i}", [128, 1024], BF16) for i in range(2)]

                def fwht_quant(nchunks, srcap, srcn, dstap, dstn, strip, alpha, sw_col):
                    nblk = nchunks // 8
                    nstrips = TOK // strip
                    nbl = strip // 128
                    for st in range(nstrips):
                        t0 = st * strip
                        for ch in range(nchunks):
                            blk, o = divmod(ch, 8)
                            sl = ch % 2
                            ps, psn = (PSA, "PSA") if sl == 0 else (PSB, "PSB")
                            for k in range(8):
                                sgn = bin(o & k).count("1") % 2
                                hp = HPM[:, 0:128] if sgn == 0 else HPM[:, 128:256]
                                em.c("pe", [srcn(blk * 8 + k), "HPM"], [f"{psn}_{tag}A{ch}"],
                                     lambda e, ps=ps, hp=hp, k=k, src=srcap(blk * 8 + k, t0, strip),
                                            strip=strip: e.matmul(ps[:, 0:strip], hp, src,
                                                                  start=(k == 0), stop=(k == 7)))
                            em.c("act", [f"{psn}_{tag}A{ch}"], [f"{tag}S{ch}"],
                                 lambda e, ps=ps, ch=ch, strip=strip: e.activation(
                                     STRIP[:, strip * ch : strip * (ch + 1)], ps[:, 0:strip], AF.Identity))
                        step = 1
                        while step < nblk:
                            for o in range(8):
                                for base in range(0, nblk, 2 * step):
                                    for j in range(step):
                                        i0 = (base + j) * 8 + o
                                        i1 = (base + j + step) * 8 + o
                                        a0 = STRIP[:, strip * i0 : strip * (i0 + 1)]
                                        a1 = STRIP[:, strip * i1 : strip * (i1 + 1)]
                                        em.c("act", [f"{tag}S{i0}"], [f"{tag}stmp"],
                                             lambda e, a0=a0, strip=strip: e.activation(
                                                 STMP[:, 0:strip], a0, AF.Identity))
                                        em.c("dve", [f"{tag}S{i0}", f"{tag}S{i1}"], [f"{tag}S{i0}"],
                                             lambda e, a0=a0, a1=a1: e.tensor_add(a0, a0, a1))
                                        em.c("pool", [f"{tag}stmp", f"{tag}S{i1}"], [f"{tag}S{i1}"],
                                             lambda e, a1=a1, strip=strip: e.tensor_sub(a1, STMP[:, 0:strip], a1))
                            step *= 2
                        em.c("dve", [f"{tag}S{ch}" for ch in range(nchunks)], [f"{tag}mx"],
                             lambda e, strip=strip, nchunks=nchunks: e.reduce_max(
                                 MXT[:, 0:strip].rearrange("p (t o) -> p t o", o=1),
                                 STRIP[:, 0 : strip * nchunks].rearrange("p (c t) -> p t c", c=nchunks),
                                 AX, apply_absolute_value=True))
                        for b in range(nbl):
                            t = t0 + 128 * b
                            em.c("pe", [f"{tag}mx", "IDF"], [f"PSC_{tag}t"],
                                 lambda e, b=b: e.transpose(PSC[:, 0:128], MXT[:, 128 * b : 128 * (b + 1)],
                                                            IDF[:, :]))
                            em.c("dve", [f"PSC_{tag}t"], [f"{tag}rc"],
                                 lambda e: e.tensor_copy(RC[:, :], PSC[:, 0:128]))
                            em.c("dve", [f"{tag}rc"], [f"{tag}mcol"],
                                 lambda e: e.reduce_max(MC[:, 0:1], RC[:, :], AX))
                            em.c("dve", [f"{tag}mcol"], [f"{tag}mc2"],
                                 lambda e: e.tensor_scalar(MC[:, 1:2], MC[:, 0:1], float(alpha),
                                                           float(Q_EPS), ALU.mult, ALU.max))
                            em.c("dve", [f"{tag}mc2"], [f"{tag}mc3"],
                                 lambda e: e.reciprocal(MC[:, 2:3], MC[:, 1:2]))
                            em.c("dve", [f"{tag}mc3"], [f"{tag}fcol"],
                                 lambda e: e.tensor_scalar_mul(MC[:, 3:4], MC[:, 2:3], float(alpha * 127.0)))
                            em.c("dve", [f"{tag}mc2", "SWB"], [f"{tag}dcol"],
                                 lambda e, sw_col=sw_col: e.scalar_tensor_tensor(
                                     MC[:, 4:5], MC[:, 1:2], float(1.0 / 127.0),
                                     SWB[:, sw_col : sw_col + 1], ALU.mult, ALU.mult))
                            col_to_row(f"{tag}fcol", MC[:, 3:4], f"{tag}frow", RQ[0:1, 0:128])
                            col_to_row(f"{tag}dcol", MC[:, 4:5], f"{tag}drow", RQ[0:1, 128:256])
                            em.c("pe", [f"{tag}frow", "ONE_R"], [f"PSC_{tag}f"],
                                 lambda e: e.matmul(PSC[:, 0:128], ONE_R[0:1, :], RQ[0:1, 0:128],
                                                    start=True, stop=True))
                            em.c("dve", [f"PSC_{tag}f"], [f"{tag}fbc"],
                                 lambda e, b=b: e.tensor_copy(FBC[:, 128 * b : 128 * (b + 1)], PSC[:, 0:128]))
                            em.c("pe", [f"{tag}drow", "ONE_R"], [f"PSC_{tag}d"],
                                 lambda e: e.matmul(PSC[:, 128:256], ONE_R[0:1, :], RQ[0:1, 128:256],
                                                    start=True, stop=True))
                            em.c("dve", [f"PSC_{tag}d"], [f"{tag}ddb"],
                                 lambda e, t=t: e.tensor_copy(DDB[:, t : t + 128], PSC[:, 128:256]))
                        for ch in range(nchunks):
                            em.c("pool", [f"{tag}S{ch}", f"{tag}fbc"], [f"{tag}S{ch}"],
                                 lambda e, ch=ch, strip=strip: e.tensor_mul(
                                     STRIP[:, strip * ch : strip * (ch + 1)],
                                     STRIP[:, strip * ch : strip * (ch + 1)], FBC[:, 0:strip]))
                            em.c("dve", [f"{tag}S{ch}"], [f"{tag}S{ch}"],
                                 lambda e, ch=ch, strip=strip: e.tensor_scalar_add(
                                     STRIP[:, strip * ch : strip * (ch + 1)],
                                     STRIP[:, strip * ch : strip * (ch + 1)], MAGIC))
                            em.c("dve", [f"{tag}S{ch}"], [dstn(ch)],
                                 lambda e, d=dstap(ch, t0, strip), ch=ch, strip=strip: e.tensor_scalar_add(
                                     d, STRIP[:, strip * ch : strip * (ch + 1)], -MAGIC))

                fwht_quant(16,
                           lambda ch, t0, w: XN[:, ch * TOK + t0 : ch * TOK + t0 + w],
                           lambda ch: f"{xnp}{ch}",
                           lambda ch, t0, w: XN[:, ch * TOK + t0 : ch * TOK + t0 + w],
                           lambda ch: f"{xnp}{ch}",
                           256, ALPHA1, 0)
                for m in range(EC):
                    ps, psn = (PSA, "PSA") if m % 2 == 0 else (PSB, "PSB")
                    for g in range(2):
                        sl = (2 * m + g) % 2
                        em.dma("Dsp", ["agu_out"], [f"{tag}wu{sl}"],
                               lambda e, m=m, sl=sl, g=g: e.dma_start(
                                   out=WUP[sl][:, :].rearrange("p (kc m) -> p kc m", kc=8),
                                   in_=agu_out.ap().rearrange("(kc p) f -> p kc f", p=128)[
                                       :, 8 * g : 8 * (g + 1), 128 * m : 128 * (m + 1)]))
                        for h in range(2):
                            cols = slice(512 * h, 512 * (h + 1))
                            for kk in range(8):
                                kc = 8 * g + kk
                                em.c("pe", [f"{xnp}{kc}", f"{tag}wu{sl}"], [f"{psn}_u{m}"],
                                     lambda e, ps=ps, sl=sl, kc=kc, kk=kk, h=h, cols=cols: e.matmul(
                                         ps[:, cols], WUP[sl][:, kk * 128 : (kk + 1) * 128],
                                         XN[:, kc * TOK + 512 * h : kc * TOK + 512 * (h + 1)],
                                         start=(kc == 0), stop=(kc == DK - 1)))
                    em.c("dve", [f"{psn}_u{m}", f"{tag}ddb"], [f"{tag}he{m}", f"{tag}HV"],
                         lambda e, ps=ps: e.tensor_mul(HEVA[:, :], ps[:, :], DDB[:, :]))
                    em.c("act", [f"{tag}he{m}"], [f"{tag}hr{m}", f"{tag}HV"],
                         lambda e: e.activation(HEVA[:, :], HEVA[:, :], AF.Relu))
                    em.c("act", [f"{tag}hr{m}", f"{tag}HV"], [f"{tag}sa{m}"],
                         lambda e, m=m: e.activation(SA[:, TOK * m : TOK * (m + 1)], HEVA[:, :], AF.Square))
                fwht_quant(32,
                           lambda ch, t0, w: SA[:, ch * TOK + t0 : ch * TOK + t0 + w],
                           lambda ch: f"{tag}sa{ch}",
                           lambda ch, t0, w: SA[:, ch * TOK + t0 : ch * TOK + t0 + w],
                           lambda ch: f"{tag}sa{ch}",
                           128, ALPHA2, 1)
                for m in range(DK):
                    ps, psn = (PSA, "PSA") if m % 2 == 0 else (PSB, "PSB")
                    for g in range(4):
                        sl = (4 * m + g) % 2
                        em.dma("Dsp", ["agd_out"], [f"{tag}wd{sl}"],
                               lambda e, m=m, sl=sl, g=g: e.dma_start(
                                   out=WDN[sl][:, :].rearrange("p (kc m) -> p kc m", kc=8),
                                   in_=agd_out.ap().rearrange("(kc p) f -> p kc f", p=128)[
                                       :, 8 * g : 8 * (g + 1), 128 * m : 128 * (m + 1)]))
                        for h in range(2):
                            cols = slice(512 * h, 512 * (h + 1))
                            for kk in range(8):
                                kc = g * 8 + kk
                                em.c("pe", [f"{tag}sa{kc}", f"{tag}wd{sl}"], [f"{psn}_d{m}"],
                                     lambda e, ps=ps, sl=sl, kc=kc, kk=kk, h=h, cols=cols: e.matmul(
                                         ps[:, cols], WDN[sl][:, kk * 128 : (kk + 1) * 128],
                                         SA[:, TOK * kc + 512 * h : TOK * kc + 512 * (h + 1)],
                                         start=(kc == 0), stop=(kc == EC - 1)))
                    em.c("dve", [f"{psn}_d{m}", f"{tag}ddb"], [f"{tag}de{m}", f"{tag}HV"],
                         lambda e, ps=ps: e.tensor_mul(HEVA[:, :], ps[:, :], DDB[:, :]))
                    em.c("dve", [f"{tag}de{m}", f"{tag}HV", f"XRES{m}"], [f"XRES{m}"],
                         lambda e, m=m: e.tensor_add(XRES[:, TOK * m : TOK * (m + 1)],
                                                     XRES[:, TOK * m : TOK * (m + 1)], HEVA[:, :]))

        def emit_attn_b(XN, xnp):
            QOFF, KOFF, VOFF, HOFF, FOFF = 0, 16 * TOK, 20 * TOK, 24 * TOK, 28 * TOK
            with contextlib.ExitStack() as s:
                ssb = lambda n, sh, d: s.enter_context(nc.sbuf_tensor(n, sh, d))
                BW = [ssb(f"BW{i}", [128, D], BF16) for i in range(2)]
                VF = ssb("VF", [128, TOK], F32)
                RS1 = ssb("RS1", [128, TOK], BF16)
                RS2 = ssb("RS2", [128, TOK], BF16)
                CMB = [ssb(f"CMB{i}", [128, 2048], BF16) for i in range(2)]
                PR = [ssb(f"PR{i}", [128, TOK], BF16) for i in range(2)]
                BCT = ssb("BCT", [128, TOK], BF16)
                ROPE = ssb("ROPE", [128, 2 * TOK], BF16)
                FMASK = ssb("FMASK", [128, 2 * TOK], BF16)
                dmain("Dpool", ROPE[:, :], rope_i[:, :], "ROPE")
                dmain("Dsp", FMASK[:, :].rearrange("p (c t) -> p c t", c=2),
                      fmask_i.rearrange("(c p) t -> p c t", p=128), "FMASK")

                def proj(wimg, nheads, wname_pref, dst_off, evac):
                    for hh in range(nheads):
                        sl = hh % 2
                        ps, psn = (PSA, "PSA") if sl == 0 else (PSB, "PSB")
                        em.dma("Dpool", [], [f"{wname_pref}{sl}", f"BWS{sl}"],
                               lambda e, hh=hh, sl=sl: e.dma_start(out=BW[sl][:, :],
                                                                   in_=wimg[:, D * hh : D * (hh + 1)]))
                        for h in range(2):
                            cols = slice(512 * h, 512 * (h + 1))
                            for kc in range(DK):
                                em.c("pe", [f"{xnp}{kc}", f"{wname_pref}{sl}", f"BWS{sl}"],
                                     [f"{psn}_{wname_pref}{hh}"],
                                     lambda e, ps=ps, sl=sl, kc=kc, h=h, cols=cols: e.matmul(
                                         ps[:, cols], BW[sl][:, kc * 128 : (kc + 1) * 128],
                                         XN[:, kc * TOK + 512 * h : kc * TOK + 512 * (h + 1)],
                                         start=(kc == 0), stop=(kc == DK - 1)))
                        evac(hh, ps, f"{psn}_{wname_pref}{hh}")

                def evac_q(hh, ps, psn):
                    em.c("act", [psn], [f"saq{hh}"],
                         lambda e, ps=ps, hh=hh: e.activation(SA[:, QOFF + TOK * hh : QOFF + TOK * (hh + 1)],
                                                              ps[:, :], AF.Identity))

                def evac_k(hh, ps, psn):
                    em.c("act", [psn], [f"sak{hh}"],
                         lambda e, ps=ps, hh=hh: e.activation(SA[:, KOFF + TOK * hh : KOFF + TOK * (hh + 1)],
                                                              ps[:, :], AF.Identity))

                def evac_v(hh, ps, psn):
                    em.c("act", [psn], ["VF"],
                         lambda e, ps=ps: e.activation(VF[:, :], ps[:, :], AF.Identity))
                    for c in range(8):
                        em.c("pe", ["VF", "IDF"], ["PSC_vt"],
                             lambda e, c=c: e.transpose(PSC[:, 0:128], VF[:, 128 * c : 128 * (c + 1)], IDF[:, :]))
                        em.c("dve", ["PSC_vt"], [f"savt{hh}_{c}"],
                             lambda e, hh=hh, c=c: e.tensor_copy(
                                 SA[:, VOFF + (hh * 8 + c) * 128 : VOFF + (hh * 8 + c + 1) * 128], PSC[:, 0:128]))

                proj(bwq_i, NH, "bq", QOFF, evac_q)
                if dbg_out is not None and os.environ.get("KDBG") == "2":
                    em.dma("Dpool", ["saq0"], ["dbqpre_o"],
                           lambda e: e.dma_start(out=dbg_out[:, 96 + 3 * TOK : 96 + 4 * TOK],
                                                 in_=SA[:, QOFF : QOFF + TOK]))
                proj(bwk_i, NKV, "bk", KOFF, evac_k)
                proj(bwv_i, NKV, "bv", VOFF, evac_v)

                def rope(tile_name_r, tile_ap):
                    for h in range(2):
                        em.c("pe", [tile_name_r, "PSW"], ["PSC_sw"],
                             lambda e, h=h: e.matmul(PSC[:, 512 * h : 512 * (h + 1)], PSW[:, :],
                                                     tile_ap[:, 512 * h : 512 * (h + 1)],
                                                     start=True, stop=True))
                    em.c("dve", [tile_name_r, "ROPE"], ["RS1"],
                         lambda e: e.tensor_mul(RS1[:, :], tile_ap, ROPE[:, 0:TOK]))
                    em.c("dve", ["PSC_sw", "ROPE"], ["RS2"],
                         lambda e: e.tensor_mul(RS2[:, :], PSC[:, :], ROPE[:, TOK : 2 * TOK]))
                    em.c("dve", ["RS1", "RS2"], [tile_name_r],
                         lambda e: e.tensor_add(tile_ap, RS1[:, :], RS2[:, :]))

                for hh in range(NH):
                    rope(f"saq{hh}", SA[:, QOFF + TOK * hh : QOFF + TOK * (hh + 1)])
                for hh in range(NKV):
                    rope(f"sak{hh}", SA[:, KOFF + TOK * hh : KOFF + TOK * (hh + 1)])
                if dbg_out is not None and os.environ.get("KDBG") == "2":
                    em.dma("Dpool", ["saq0"], ["dbq_o"],
                           lambda e: e.dma_start(out=dbg_out[:, 96 : 96 + TOK], in_=SA[:, QOFF : QOFF + TOK]))
                    em.dma("Dpool", ["sak0"], ["dbk_o"],
                           lambda e: e.dma_start(out=dbg_out[:, 96 + TOK : 96 + 2 * TOK], in_=SA[:, KOFF : KOFF + TOK]))
                    em.dma("Dpool", [f"savt0_{c}" for c in range(8)], ["dbv_o"],
                           lambda e: e.dma_start(out=dbg_out[:, 96 + 2 * TOK : 96 + 3 * TOK], in_=SA[:, VOFF : VOFF + TOK]))
                for kvh in range(NKV):
                    em.dma("Dsp", [f"sak{kvh}"], [f"kvi_k{kvh}"],
                           lambda e, kvh=kvh: e.dma_start(out=kv_in.ap()[:, 512 * kvh : 512 * (kvh + 1)],
                                                          in_=SA[:, KOFF + TOK * kvh + 512 : KOFF + TOK * (kvh + 1)]))
                    em.dma("Dsp", [f"sak{kvh}"], [f"kvi_kf{kvh}"],
                           lambda e, kvh=kvh: e.dma_start(out=kv_in.ap()[:, 2048 + 256 * kvh : 2048 + 256 * (kvh + 1)],
                                                          in_=SA[:, KOFF + TOK * kvh : KOFF + TOK * kvh + 256]))
                    em.dma("Dsp", [f"savt{kvh}_{c}" for c in range(4, 8)], [f"kvi_v{kvh}"],
                           lambda e, kvh=kvh: e.dma_start(
                               out=kv_in.ap()[:, 3072 + 512 * kvh : 3072 + 512 * (kvh + 1)],
                               in_=SA[:, VOFF + (kvh * 8 + 4) * 128 : VOFF + (kvh * 8 + 8) * 128]))
                    em.dma("Dsp", [f"savt{kvh}_{c}" for c in range(2)], [f"kvi_vf{kvh}"],
                           lambda e, kvh=kvh: e.dma_start(
                               out=kv_in.ap()[:, 5120 + 256 * kvh : 5120 + 256 * (kvh + 1)],
                               in_=SA[:, VOFF + (kvh * 8) * 128 : VOFF + (kvh * 8 + 2) * 128]))
                em.cc([f"kvi_k{k}" for k in range(NKV)] + [f"kvi_kf{k}" for k in range(NKV)]
                      + [f"kvi_v{k}" for k in range(NKV)] + [f"kvi_vf{k}" for k in range(NKV)],
                      ["kv_out"],
                      lambda e: e.collective_compute("AllGather", ALU.bypass,
                                                     replica_groups=[list(range(NCORES))],
                                                     ins=[kv_in.ap().opt()], outs=[kv_out.ap().opt()]))
                for part, (c0, cw) in enumerate([(0, 2048), (3072, 2048)]):
                    hoff = HOFF + part * 2048
                    for r in range(NCORES):
                        sl = r % 2
                        em.dma("Dsp", ["kv_out"], [f"cmb{sl}"],
                               lambda e, r=r, sl=sl, c0=c0, cw=cw: e.dma_start(
                                   out=CMB[sl][:, 0:cw], in_=kv_out.ap()[128 * r : 128 * (r + 1), c0 : c0 + cw]))
                        if r == 0:
                            em.c("dve", [f"cmb{sl}", "CSEL"], [f"halo{part}"],
                                 lambda e, sl=sl, hoff=hoff, cw=cw: e.tensor_scalar_mul(
                                     SA[:, hoff : hoff + cw], CMB[sl][:, 0:cw], CSEL[:, 0:1]))
                        else:
                            em.c("dve", [f"cmb{sl}", "CSEL", f"halo{part}"], [f"halo{part}"],
                                 lambda e, sl=sl, r=r, hoff=hoff, cw=cw: e.scalar_tensor_tensor(
                                     SA[:, hoff : hoff + cw], CMB[sl][:, 0:cw], CSEL[:, r : r + 1],
                                     SA[:, hoff : hoff + cw], ALU.mult, ALU.add))
                em.dma("Dsp", ["kv_out"], ["front_k"],
                       lambda e: e.dma_start(out=SA[:, FOFF : FOFF + 1024], in_=kv_out.ap()[0:128, 2048:3072]))
                em.dma("Dsp", ["kv_out"], ["front_v"],
                       lambda e: e.dma_start(out=SA[:, FOFF + 1024 : FOFF + 2048], in_=kv_out.ap()[0:128, 5120:6144]))

                for hh in range(NH):
                    kvh = hh // 4
                    qname = f"saq{hh}"
                    qap = lambda q0, w: SA[:, QOFF + TOK * hh + q0 : QOFF + TOK * hh + q0 + w]
                    srcs = []
                    for j in range(4):
                        srcs.append(("halo", j,
                                     SA[:, HOFF + kvh * 512 + 128 * j : HOFF + kvh * 512 + 128 * (j + 1)],
                                     SA[:, HOFF + 2048 + (kvh * 4 + j) * 128 : HOFF + 2048 + (kvh * 4 + j + 1) * 128],
                                     0, (j + 1) * 128, [f"halo0", f"halo1"]))
                    for c in range(8):
                        q0 = c * 128
                        qw = (min(c + 4, 7) - c + 1) * 128
                        srcs.append(("local", c,
                                     SA[:, KOFF + TOK * kvh + 128 * c : KOFF + TOK * kvh + 128 * (c + 1)],
                                     SA[:, VOFF + (kvh * 8 + c) * 128 : VOFF + (kvh * 8 + c + 1) * 128],
                                     q0, qw, [f"sak{kvh}", f"savt{kvh}_{c}"]))
                    for fc in range(2):
                        srcs.append(("front", fc,
                                     SA[:, FOFF + kvh * 256 + 128 * fc : FOFF + kvh * 256 + 128 * (fc + 1)],
                                     SA[:, FOFF + 1024 + (kvh * 2 + fc) * 128 : FOFF + 1024 + (kvh * 2 + fc + 1) * 128],
                                     0, TOK, ["front_k", "front_v"]))

                    av_started_banks = set()
                    dn_started_banks = set()
                    nsrc = len(srcs)
                    for si, (kind, idx, kap, vap, q0, qw, knames) in enumerate(srcs):
                        sps, spn = (PSA, "PSA") if si % 2 == 0 else (PSC, "PSC")
                        prn = si % 2
                        for o in range(0, qw, 512):
                            w = min(512, qw - o)
                            em.c("pe", [qname] + knames, [f"{spn}_sc{hh}_{si}"],
                                 lambda e, sps=sps, kap=kap, o=o, w=w, rhs=qap(q0 + o, w): e.matmul(
                                     sps[:, o : o + w], kap, rhs, start=True, stop=True))
                        em.c("act", [f"{spn}_sc{hh}_{si}", "CPOOL"], [f"pr{prn}"],
                             lambda e, sps=sps, prn=prn, qw=qw: e.activation(
                                 PR[prn][:, 0:qw], sps[:, 0:qw], AF.Exp, bias=SM_SHIFT, scale=SM_SCALE))
                        if kind == "local":
                            em.c("dve", [f"pr{prn}", "TRIM"], [f"pr{prn}"],
                                 lambda e, prn=prn: e.tensor_mul(PR[prn][:, 0:128], PR[prn][:, 0:128],
                                                                 TRIM[:, 0:128]))
                            if idx <= 3:
                                em.c("dve", [f"pr{prn}", "TRIM"], [f"pr{prn}"],
                                     lambda e, prn=prn, qw=qw: e.tensor_mul(PR[prn][:, qw - 128 : qw],
                                                                            PR[prn][:, qw - 128 : qw],
                                                                            TRIM[:, 128:256]))
                        elif kind == "halo":
                            em.c("dve", [f"pr{prn}", "CSEL"], [f"pr{prn}"],
                                 lambda e, prn=prn, qw=qw: e.tensor_scalar_mul(PR[prn][:, 0:qw],
                                                                               PR[prn][:, 0:qw], CSEL[:, 16:17]))
                            em.c("dve", [f"pr{prn}", "TRIM"], [f"pr{prn}"],
                                 lambda e, prn=prn, qw=qw: e.tensor_mul(PR[prn][:, qw - 128 : qw],
                                                                        PR[prn][:, qw - 128 : qw],
                                                                        TRIM[:, 128:256]))
                        else:
                            em.c("dve", [f"pr{prn}", "FMASK"], [f"pr{prn}"],
                                 lambda e, prn=prn, idx=idx: e.tensor_mul(PR[prn][:, 0:TOK], PR[prn][:, 0:TOK],
                                                                          FMASK[:, TOK * idx : TOK * (idx + 1)]))
                        a = q0
                        while a < q0 + qw:
                            b_end = min(q0 + qw, (a // 512 + 1) * 512)
                            w = b_end - a
                            bank = a // 512
                            o = a - q0
                            psr = PSR0 if bank == 0 else PSR1
                            dst = a % 512
                            st = bank not in dn_started_banks
                            dn_started_banks.add(bank)
                            em.c("pe", [f"pr{prn}", "ONE_B"], [f"PSR{bank}_dn{hh}"],
                                 lambda e, psr=psr, dst=dst, w=w, prn=prn, o=o, st=st: e.matmul(
                                     psr[0:1, dst : dst + w], ONE_B[:, :], PR[prn][:, o : o + w],
                                     start=st, stop=(si == nsrc - 1)))
                            stb = bank not in av_started_banks
                            av_started_banks.add(bank)
                            em.c("pe", [f"pr{prn}"] + knames, [f"PSB_av{hh}_{bank}"],
                                 lambda e, vap=vap, prn=prn, o=o, w=w, a=a, stb=stb: e.matmul(
                                     PSB[:, a : a + w], vap, PR[prn][:, o : o + w],
                                     start=stb, stop=False))
                            a = b_end
                    em.c("dve", [f"PSR0_dn{hh}", f"PSR1_dn{hh}", "ROWSBUF"], [f"drw{hh}", "ROWSBUF"],
                         lambda e: (e.tensor_copy(ROWS[0:1, 0:512], PSR0[0:1, :]),
                                    e.tensor_copy(ROWS[0:1, 512:1024], PSR1[0:1, :]))[1])
                    em.c("dve", [f"drw{hh}"], [f"drr{hh}", "ROWSBUF"],
                         lambda e: e.reciprocal(ROWS[0:1, 0:1024], ROWS[0:1, 0:1024]))
                    for o in range(0, TOK, 512):
                        em.c("pe", [f"drr{hh}", "ONE_R"], [f"PSC_nb{hh}"],
                             lambda e, o=o: e.matmul(PSC[:, o : o + 512], ONE_R[0:1, :],
                                                     ROWS[0:1, o : o + 512], start=True, stop=True))
                    em.c("dve", [f"PSC_nb{hh}"], [f"bct{hh}"],
                         lambda e: e.tensor_copy(BCT[:, :], PSC[:, :]))
                    em.c("dve", [f"bct{hh}", f"PSB_av{hh}_0", f"PSB_av{hh}_1"], [f"{xnp}{hh}"],
                         lambda e, hh=hh: e.tensor_mul(XN[:, TOK * hh : TOK * (hh + 1)], PSB[:, :], BCT[:, :]))

                if dbg_out is not None and os.environ.get("KDBG") == "2":
                    em.dma("Dpool", [f"{xnp}0"], ["dbs_o"],
                           lambda e: e.dma_start(out=dbg_out[:, 0:TOK] if False else dbg_out[:, 96 + 2 * TOK : 96 + 3 * TOK],
                                                 in_=XN[:, 0:TOK]))
                for m in range(DK):
                    sl = m % 2
                    ps, psn = (PSA, "PSA") if sl == 0 else (PSB, "PSB")
                    em.dma("Dpool", [], [f"bo{sl}", f"BWS{sl}"],
                           lambda e, m=m, sl=sl: e.dma_start(out=BW[sl][:, :], in_=bwo_i[:, D * m : D * (m + 1)]))
                    for h in range(2):
                        cols = slice(512 * h, 512 * (h + 1))
                        for k in range(NH):
                            em.c("pe", [f"{xnp}{k}", f"bo{sl}", f"BWS{sl}"], [f"{psn}_bo{m}"],
                                 lambda e, ps=ps, sl=sl, k=k, h=h, cols=cols: e.matmul(
                                     ps[:, cols], BW[sl][:, k * 128 : (k + 1) * 128],
                                     XN[:, TOK * k + 512 * h : TOK * k + 512 * (h + 1)],
                                     start=(k == 0), stop=(k == NH - 1)))
                    em.c("dve", [f"{psn}_bo{m}", f"XRES{m}"], [f"XRES{m}"],
                         lambda e, ps=ps, m=m: e.tensor_add(XRES[:, TOK * m : TOK * (m + 1)],
                                                            XRES[:, TOK * m : TOK * (m + 1)], ps[:, :]))

        sub = 0
        if sub < n_sub:
            with contextlib.ExitStack() as xs:
                XN1 = xs.enter_context(nc.sbuf_tensor("XN1", [128, DK * TOK], BF16))
                emit_ln("L1", XN1, "xn1_")
                emit_attn_a(XN1, "xn1_")
            em.barrier()
        sub += 1
        if sub < n_sub:
            with contextlib.ExitStack() as xs:
                XN2 = xs.enter_context(nc.sbuf_tensor("XN2", [128, DK * TOK], BF16))
                emit_ln("L2", XN2, "xn2_")
                emit_ffn(XN2, "xn2_", "f1")
            em.barrier()
        sub += 1
        if sub < n_sub:
            with contextlib.ExitStack() as xs:
                XN3 = xs.enter_context(nc.sbuf_tensor("XN3", [128, DK * TOK], BF16))
                emit_ln("L3", XN3, "xn3_")
                emit_attn_b(XN3, "xn3_")
            em.barrier()
        sub += 1
        if sub < n_sub:
            with contextlib.ExitStack() as xs:
                XN4 = xs.enter_context(nc.sbuf_tensor("XN4", [128, DK * TOK], BF16))
                emit_ln("L4", XN4, "xn4_")
                emit_ffn(XN4, "xn4_", "f2")
            em.barrier()

        em.dma("Dsp", [f"XRES{kc}" for kc in range(DK)], ["y_out"],
               lambda e: e.dma_start(out=y_out.rearrange("(c p) t -> p c t", p=128), in_=XRES[:, :].rearrange("p (c t) -> p c t", c=DK)))
        em.barrier()
        em.c("sp", [], [], lambda e: e.nop())

        em.finalize()

        sems = {}
        with contextlib.ExitStack() as semctx:
            for name in ["pe", "act", "dve", "pool", "sp", "cc", "Dsp", "Dact", "Dpool"]:
                sems[name] = semctx.enter_context(nc.semaphore(f"sem_{name}"))
            with nc.Block() as block:
                @block.tensor
                def _(e):
                    em.replay("pe", e, sems)

                @block.scalar
                def _(e):
                    em.replay("act", e, sems)

                @block.vector
                def _(e):
                    em.replay("dve", e, sems)

                @block.gpsimd
                def _(e):
                    em.replay("pool", e, sems)

                @block.sync
                def _(e):
                    em.replay("sp", e, sems)

    return nc


def _prep_shared(inp):
    W = {}
    W["wq1"] = pack_kc(np.asarray(inp["L0_attn_a_Wq1"], np.float32))
    W["wk1"] = pack_kc(np.asarray(inp["L0_attn_a_Wk1"], np.float32))
    wq2 = np.asarray(inp["L0_attn_a_Wq2"], np.float32)
    wk2 = np.asarray(inp["L0_attn_a_Wk2"], np.float32)
    wv = np.asarray(inp["L0_attn_a_Wv"], np.float32)
    aw1 = np.empty((128, EC * 2048), np.float32)
    aw2 = np.empty((128, EC * 256), np.float32)
    for c in range(EC):
        aw1[:, 2048 * c : 2048 * (c + 1)] = pack_kc(wv[:, 128 * c : 128 * (c + 1)])
        aw2[:, 256 * c : 256 * c + 128] = wq2[:, 128 * c : 128 * (c + 1)]
        aw2[:, 256 * c + 128 : 256 * c + 256] = wk2[:, 128 * c : 128 * (c + 1)]
    W["aw1"], W["aw2"] = aw1, aw2
    W["wo_a"] = pack_mc(np.asarray(inp["L0_attn_a_Wo"], np.float32))
    W["nw_fm"] = np.ascontiguousarray(
        np.asarray(inp["L0_attn_a_norm_w"], np.float32).reshape(EC, 128).T)
    W["up_img"] = np.ascontiguousarray(np.asarray(inp["L0_ffn_w_up"], np.float32).T)
    W["dn_img"] = np.ascontiguousarray(np.asarray(inp["L0_ffn_w_down"], np.float32).T)

    perm = np.concatenate([np.concatenate([h * 128 + np.arange(0, 128, 2),
                                           h * 128 + np.arange(1, 128, 2)])
                           for h in range(NH)])
    permk = perm[: NKV * 128]
    bwq = np.asarray(inp["L0_attn_b_wq"], np.float32)[:, perm]
    bwk = np.asarray(inp["L0_attn_b_wk"], np.float32)[:, permk]
    bwv = np.asarray(inp["L0_attn_b_wv"], np.float32)
    W["bwq"] = np.concatenate([pack_kc(bwq[:, 128 * h : 128 * (h + 1)]) for h in range(NH)], 1)
    W["bwk"] = np.concatenate([pack_kc(bwk[:, 128 * h : 128 * (h + 1)]) for h in range(NKV)], 1)
    W["bwv"] = np.concatenate([pack_kc(bwv[:, 128 * h : 128 * (h + 1)]) for h in range(NKV)], 1)
    W["bwo"] = pack_mc(np.asarray(inp["L0_attn_b_wo"], np.float32))

    h128 = hadamard128()
    W["hpm"] = np.concatenate([h128, -h128], 1).astype(BF)
    jj = np.arange(128)[:, None]
    qq = np.arange(128)[None, :]
    W["trim"] = np.concatenate([(jj <= qq).astype(np.float32),
                                (jj > qq).astype(np.float32)], 1).astype(BF)
    psw = np.zeros((128, 128), np.float32)
    psw[np.arange(64), np.arange(64) + 64] = 1.0
    psw[np.arange(64) + 64, np.arange(64)] = 1.0
    W["psw"] = psw.astype(BF)
    W["idf"] = np.eye(128, dtype=np.float32)
    return W


def _prep_core(c):
    P = {}
    pos = np.arange(TOK * c, TOK * (c + 1))
    inv_freq = 1.0 / 10000.0 ** (np.arange(0, DQK_ := 128, 2)[:64].astype(np.float32) / 128)
    ang = np.outer(pos.astype(np.float32), inv_freq)
    cos = np.cos(ang).astype(np.float32).T
    sin = np.sin(ang).astype(np.float32).T
    c2 = np.concatenate([cos, cos], 0)
    s2 = np.concatenate([-sin, sin], 0)
    P["ropetab"] = np.ascontiguousarray(np.concatenate([c2, s2], 1))
    j = np.arange(256)[:, None]
    i = pos[None, :]
    fm = ((j < 256) & (j <= i - 512)).astype(np.float32)
    P["fmask"] = np.ascontiguousarray(fm).astype(BF)
    csel = np.zeros((128, 24), np.float32)
    if c > 0:
        csel[:, (c - 1) % NCORES] = 1.0
        csel[:, 16] = 1.0
    for r in range(NCORES):
        if r < c:
            csel[:, 8 + r] = 1.0
    P["csel"] = csel
    return P


_NC_CACHE = {}


def kernel(**inputs):
    n_sub = int(inputs.pop("_n_sub", 4))
    x = np.asarray(inputs["L0_attn_a_x"], np.float32)
    W = _prep_shared(inputs)
    if n_sub not in _NC_CACHE:
        _NC_CACHE[n_sub] = build_program(n_sub)
    nc = _NC_CACHE[n_sub]

    in_maps = []
    for c in range(NCORES):
        pc = _prep_core(c)
        m = {
            "x_fm": np.ascontiguousarray(x[0, TOK * c : TOK * (c + 1), :].T),
            "wq1": W["wq1"], "wk1": W["wk1"], "aw1": W["aw1"], "aw2": W["aw2"],
            "wo_a": W["wo_a"], "nw_fm": W["nw_fm"],
            "wq_up_sl": np.ascontiguousarray(W["up_img"][256 * c : 256 * (c + 1), :]),
            "wq_dn_sl": np.ascontiguousarray(W["dn_img"][512 * c : 512 * (c + 1), :]),
            "bwq": W["bwq"], "bwk": W["bwk"], "bwv": W["bwv"], "bwo": W["bwo"],
            "hpm": W["hpm"], "trim": W["trim"], "psw": W["psw"], "idf": W["idf"],
            "ropetab": pc["ropetab"], "fmask": pc["fmask"], "csel": pc["csel"],
        }
        in_maps.append(m)

    res = run_bass_kernel_spmd(nc, in_maps, core_ids=list(range(NCORES)))
    out = np.empty((1, NCORES * TOK, D), np.float32)
    for c in range(NCORES):
        out[0, TOK * c : TOK * (c + 1), :] = np.asarray(res.results[c]["y_fm"]).T
    if os.environ.get("KDBG"):
        kernel._dbg = [np.asarray(res.results[c].get("dbg")) for c in range(NCORES)]
    return out


# version 02
# speedup vs baseline: 1.0017x; 1.0017x over previous
import contextlib
import math
import os

import numpy as np
import ml_dtypes

import concourse.bass as bass
import concourse.mybir as mybir
from concourse.bass_utils import run_bass_kernel_spmd

F32 = mybir.dt.float32
BF16 = mybir.dt.bfloat16
AF = mybir.ActivationFunctionType
ALU = mybir.AluOpType
AX = mybir.AxisListType.X

NCORES = 8
TOK = 1024
D = 2048
DK = 16
E2 = 4096
EC = 32
NH = 16
NKV = 4

MAGIC = float(np.float32(12582912.0))
LN_EPS = 1e-5
RMS_EPS = 1e-6
Q_EPS = 1e-5
ALPHA1 = float(1.0 / math.sqrt(2048.0))
ALPHA2 = float(1.0 / math.sqrt(4096.0))
SM_SCALE = float(1.0 / math.sqrt(128.0))
SM_SHIFT = -20.0

BF = ml_dtypes.bfloat16


class Em:
    ENGINES = ("pe", "act", "dve", "pool", "sp")
    QUEUES = {"Dsp": "sp", "Dact": "act", "Dpool": "pool"}

    def __init__(self):
        self.ops = []
        self.last_w = {}
        self.readers = {}
        self.bar = set()

    PHYS = ("PSA", "PSB", "PSC", "PSR0", "PSR1")

    @classmethod
    def _expand(cls, names):
        out = list(names)
        for b in names:
            for p in cls.PHYS:
                if b.startswith(p):
                    out.append(p)
                    break
        return out

    def _rec(self, eng, sem, step, fn, reads, writes):
        reads = self._expand(reads)
        writes = self._expand(writes)
        deps = set(self.bar)
        for b in reads:
            if b in self.last_w:
                deps.add(self.last_w[b])
        for b in writes:
            if b in self.last_w:
                deps.add(self.last_w[b])
            for r in self.readers.get(b, ()):
                deps.add(r)
        i = len(self.ops)
        deps.discard(i)
        self.ops.append(dict(eng=eng, sem=sem, step=step, fn=fn, deps=deps))
        for b in reads:
            self.readers.setdefault(b, []).append(i)
        for b in writes:
            self.last_w[b] = i
            self.readers[b] = [i]
        return i

    def c(self, eng, reads, writes, fn):
        return self._rec(eng, eng, 1, fn, reads, writes)

    def dma(self, q, reads, writes, fn):
        return self._rec(self.QUEUES[q], q, 16, fn, reads, writes)

    def cc(self, reads, writes, fn):
        return self._rec("pool", "cc", 1, fn, reads, writes)

    def barrier(self):
        keep = {}
        for i, o in enumerate(self.ops):
            keep[(o["eng"], o["sem"])] = i
        self.bar = set(keep.values())

    def finalize(self):
        needed = set()
        for o in self.ops:
            needed |= o["deps"]
        lastsem = {}
        for i, o in enumerate(self.ops):
            lastsem[o["sem"]] = i
        needed |= set(lastsem.values())
        semval = {}
        for i, o in enumerate(self.ops):
            if o["sem"] in self.QUEUES or o["sem"] == "cc" or i in needed:
                semval[o["sem"]] = semval.get(o["sem"], 0) + o["step"]
                o["inc"], o["val"] = True, semval[o["sem"]]
            else:
                o["inc"], o["val"] = False, None
        for o in self.ops:
            w = {}
            for d in o["deps"]:
                p = self.ops[d]
                if p["val"] is None:
                    raise RuntimeError("dep target not marked")
                w[p["sem"]] = max(w.get(p["sem"], 0), p["val"])
            o["waits"] = w
        self.final_vals = dict(semval)
        qs = {e: [] for e in self.ENGINES}
        for i, o in enumerate(self.ops):
            qs[o["eng"]].append(i)
        pos = {e: 0 for e in self.ENGINES}
        cur = {}
        prog, ndone = True, 0
        while prog:
            prog = False
            for e in self.ENGINES:
                while pos[e] < len(qs[e]):
                    o = self.ops[qs[e][pos[e]]]
                    if all(cur.get(s, 0) >= v for s, v in o["waits"].items()):
                        if o["inc"]:
                            cur[o["sem"]] = cur.get(o["sem"], 0) + o["step"]
                        pos[e] += 1
                        ndone += 1
                        prog = True
                    else:
                        break
        if ndone != len(self.ops):
            stuck = {e: pos[e] for e in self.ENGINES if pos[e] < len(qs[e])}
            raise RuntimeError(f"schedule deadlock, stuck={stuck}, done {ndone}/{len(self.ops)}")

    def replay(self, eng, h, sems):
        for o in self.ops:
            if o["eng"] != eng:
                continue
            for s, v in sorted(o["waits"].items()):
                h.wait_ge(sems[s], v)
            inst = o["fn"](h)
            if o["inc"]:
                inst.then_inc(sems[o["sem"]], o["step"])


def pack_kc(w):
    K, M = w.shape
    return np.ascontiguousarray(
        np.transpose(w.reshape(K // 128, 128, M), (1, 0, 2)).reshape(128, -1))


def pack_mc(w, mt=128):
    K, M = w.shape
    a = w.reshape(K // 128, 128, M // mt, mt)
    return np.ascontiguousarray(np.transpose(a, (1, 2, 0, 3)).reshape(128, -1))


def hadamard128():
    i = np.arange(128)
    a = i[:, None] & i[None, :]
    pc = np.zeros((128, 128), np.int64)
    v = a.copy()
    while v.any():
        pc += v & 1
        v >>= 1
    return np.where(pc % 2 == 0, 1.0, -1.0).astype(np.float32)


def build_program(n_sub=4):
    nc = bass.Bass("TRN2", target_bir_lowering=False, debug=False)

    x_in = nc.dram_tensor("x_fm", [D, TOK], F32, kind="ExternalInput").ap()
    wq1_i = nc.dram_tensor("wq1", [128, D], F32, kind="ExternalInput").ap()
    wk1_i = nc.dram_tensor("wk1", [128, D], F32, kind="ExternalInput").ap()
    aw1_i = nc.dram_tensor("aw1", [128, EC * 2048], F32, kind="ExternalInput").ap()
    aw2_i = nc.dram_tensor("aw2", [128, EC * 256], F32, kind="ExternalInput").ap()
    wo_i = nc.dram_tensor("wo_a", [128, DK * E2], F32, kind="ExternalInput").ap()
    nw_i = nc.dram_tensor("nw_fm", [128, EC], F32, kind="ExternalInput").ap()
    wqup_sl = nc.dram_tensor("wq_up_sl", [256, E2], F32, kind="ExternalInput").ap()
    wqdn_sl = nc.dram_tensor("wq_dn_sl", [512, D], F32, kind="ExternalInput").ap()
    bwq_i = nc.dram_tensor("bwq", [128, NH * D], F32, kind="ExternalInput").ap()
    bwk_i = nc.dram_tensor("bwk", [128, NKV * D], F32, kind="ExternalInput").ap()
    bwv_i = nc.dram_tensor("bwv", [128, NKV * D], F32, kind="ExternalInput").ap()
    bwo_i = nc.dram_tensor("bwo", [128, DK * D], F32, kind="ExternalInput").ap()
    hpm_i = nc.dram_tensor("hpm", [128, 256], BF16, kind="ExternalInput").ap()
    trim_i = nc.dram_tensor("trim", [128, 256], BF16, kind="ExternalInput").ap()
    psw_i = nc.dram_tensor("psw", [128, 128], BF16, kind="ExternalInput").ap()
    idf_i = nc.dram_tensor("idf", [128, 128], F32, kind="ExternalInput").ap()
    rope_i = nc.dram_tensor("ropetab", [128, 2 * TOK], F32, kind="ExternalInput").ap()
    fmask_i = nc.dram_tensor("fmask", [256, TOK], BF16, kind="ExternalInput").ap()
    csel_i = nc.dram_tensor("csel", [128, 24], F32, kind="ExternalInput").ap()

    y_out = nc.dram_tensor("y_fm", [D, TOK], F32, kind="ExternalOutput").ap()
    dbg_out = nc.dram_tensor("dbg", [128, 96 + 4 * TOK], F32, kind="ExternalOutput").ap() if os.environ.get("KDBG") else None

    ar_in = nc.dram_tensor("ar_in", [1, 2], F32)
    ar_out = nc.dram_tensor("ar_out", [1, 2], F32)
    agu_in = nc.dram_tensor("agu_in", [256, E2], BF16)
    agu_out = nc.dram_tensor("agu_out", [D, E2], BF16)
    agd_in = nc.dram_tensor("agd_in", [512, D], BF16)
    agd_out = nc.dram_tensor("agd_out", [E2, D], BF16)
    car_in = nc.dram_tensor("car_in", [128, 64], F32)
    car_out = nc.dram_tensor("car_out", [NCORES * 128, 64], F32)
    kv_in = nc.dram_tensor("kv_in", [128, 6144], BF16)
    kv_out = nc.dram_tensor("kv_out", [NCORES * 128, 6144], BF16)

    em = Em()
    ctx = contextlib.ExitStack()
    with ctx:
        sb = lambda n, s, d: ctx.enter_context(nc.sbuf_tensor(n, s, d))

        XRES = sb("XRES", [128, DK * TOK], F32)
        SA = sb("SA", [128, EC * TOK], BF16)
        HPM = sb("HPM", [128, 256], BF16)
        TRIM = sb("TRIM", [128, 256], BF16)
        PSW = sb("PSW", [128, 128], BF16)
        IDF = sb("IDF", [128, 128], F32)
        NW = sb("NW", [128, EC], F32)
        CSEL = sb("CSEL", [128, 24], F32)
        ONE_F = sb("ONE_F", [128, 1], F32)
        ONE_B = sb("ONE_B", [128, 1], BF16)
        ONE_R = sb("ONE_R", [1, 128], F32)
        CPOOL = sb("CPOOL", [128, 8], F32)
        ROWS = sb("ROWS", [1, 1536], F32)
        SROW = sb("SROW", [1, 16], F32)
        SWB = sb("SWB", [128, 4], F32)

        PSA = ctx.enter_context(nc.psum_tensor("PSA", [128, TOK], F32))
        PSB = ctx.enter_context(nc.psum_tensor("PSB", [128, TOK], F32))
        PSC = ctx.enter_context(nc.psum_tensor("PSC", [128, TOK], F32))
        PSR0 = ctx.enter_context(nc.psum_tensor("PSR0", [1, 512], F32))
        PSR1 = ctx.enter_context(nc.psum_tensor("PSR1", [1, 512], F32))

        CONSTS = [LN_EPS, RMS_EPS, SM_SHIFT, -0.5, float(1.0 / E2), SM_SCALE]
        for k, v in enumerate(CONSTS):
            nc.const_aps.register(F32, float(v), CPOOL[:, k : k + 1])

        def memset(ap, val, name):
            em.c("dve", [], [name], lambda e, a=ap, v=val: e.memset(a, v))

        def dmain(q, dst, src, wname, reads=()):
            em.dma(q, list(reads), [wname], lambda e, d=dst, s=src: e.dma_start(out=d, in_=s))

        memset(ONE_F[:, :], 1.0, "ONE_F")
        memset(ONE_B[:, :], 1.0, "ONE_B")
        memset(ONE_R[:, :], 1.0, "ONE_R")
        for k, v in enumerate(CONSTS):
            em.c("dve", [], ["CPOOL"], lambda e, k=k, v=v: e.memset(CPOOL[:, k : k + 1], float(v)))
        dmain("Dsp", XRES[:, :].rearrange("p (c t) -> p c t", c=DK), x_in.rearrange("(c p) t -> p c t", p=128), "XRES_all")
        for kc in range(DK):
            em.c("dve", ["XRES_all"], [f"XRES{kc}"], lambda e: e.engine_nop())
        dmain("Dsp", HPM[:, :], hpm_i[:, :], "HPM")
        dmain("Dsp", TRIM[:, :], trim_i[:, :], "TRIM")
        dmain("Dsp", PSW[:, :], psw_i[:, :], "PSW")
        dmain("Dsp", IDF[:, :], idf_i[:, :], "IDF")
        dmain("Dsp", NW[:, :], nw_i[:, :], "NW")
        dmain("Dsp", CSEL[:, :], csel_i[:, :], "CSEL")

        QMODE = int(os.environ.get("KQM", "3"))
        with contextlib.ExitStack() as qsx:
            qsb = lambda n, s, d: qsx.enter_context(nc.sbuf_tensor(n, s, d))
            W2 = [qsb(f"W2_{i}", [128, D], F32) for i in range(2)]
            QOUT = qsb("QOUT", [128, D], BF16)
            QT = qsb("QT", [128, D], F32)
            QSUM = qsb("QSUM", [128, 12], F32)

            halves = ([("u", i, hf, wqup_sl[128 * i : 128 * (i + 1), 2048 * hf : 2048 * (hf + 1)],
                        agu_in.ap()[128 * i : 128 * (i + 1), 2048 * hf : 2048 * (hf + 1)])
                       for i in range(2) for hf in range(2)]
                      + [("d", j, 0, wqdn_sl[128 * j : 128 * (j + 1), :],
                          agd_in.ap()[128 * j : 128 * (j + 1), :]) for j in range(4)])
            halves = list(halves)
            for t, (kind, i, hf, srcap, dstap) in enumerate(halves):
                sl = t % 2
                dmain("Dsp", W2[sl][:, :], srcap, f"W2_{sl}")
                em.c("dve", [f"W2_{sl}"], [f"QS{t}"],
                     lambda e, t=t, sl=sl: e.reduce_sum(QSUM[:, t : t + 1], W2[sl][:, :], AX,
                                                        apply_absolute_value=True))
            em.c("pe", [f"QS{t}" for t in range(8)] + ["ONE_F"], ["PSR0_q"],
                 lambda e: e.matmul(PSR0[0:1, 0:12], ONE_F[:, :], QSUM[:, 0:12], start=True, stop=True))
            em.c("dve", ["PSR0_q"], ["SROWq"],
                 lambda e: e.tensor_copy(SROW[0:1, 0:12], PSR0[0:1, 0:12]))
            em.c("dve", ["SROWq"], ["SROWu0"],
                 lambda e: e.tensor_add(SROW[0:1, 12:14], SROW[0:1, 0:2], SROW[0:1, 2:4]))
            em.c("dve", ["SROWu0"], ["SROWu"],
                 lambda e: e.tensor_add(SROW[0:1, 8:9], SROW[0:1, 12:13], SROW[0:1, 13:14]))
            em.c("dve", ["SROWq"], ["SROWd0"],
                 lambda e: e.tensor_add(SROW[0:1, 12:14], SROW[0:1, 4:6], SROW[0:1, 6:8]))
            em.c("dve", ["SROWd0"], ["SROWd"],
                 lambda e: e.tensor_add(SROW[0:1, 9:10], SROW[0:1, 12:13], SROW[0:1, 13:14]))
            em.dma("Dsp", ["SROWu", "SROWd"], ["ar_in"],
                   lambda e: e.dma_start(out=ar_in.ap()[0:1, 0:2], in_=SROW[0:1, 8:10]))
            if QMODE >= 2:
                em.cc(["ar_in"], ["ar_out"],
                      lambda e: e.collective_compute("AllReduce", ALU.add,
                                                     replica_groups=[list(range(NCORES))],
                                                     ins=[ar_in.ap().opt()], outs=[ar_out.ap().opt()]))
                dmain("Dsp", SROW[0:1, 10:12], ar_out.ap()[0:1, 0:2], "SROW_AR", reads=["ar_out"])
            else:
                em.c("dve", ["SROWu", "SROWd"], ["SROW_AR"],
                     lambda e: e.tensor_copy(SROW[0:1, 10:12], SROW[0:1, 8:10]))
            em.c("dve", ["SROW_AR"], ["SROW_SW"],
                 lambda e: e.tensor_scalar(SROW[0:1, 12:14], SROW[0:1, 10:12],
                                           float(1.0 / (E2 * D)), float(Q_EPS), ALU.mult, ALU.max))
            em.c("dve", ["SROW_SW"], ["SROW_SI"],
                 lambda e: e.reciprocal(SROW[0:1, 14:16], SROW[0:1, 12:14]))
            em.c("pe", ["SROW_SI", "SROW_SW", "ONE_R"], ["PSC_sw"],
                 lambda e: e.matmul(PSC[:, 0:4], ONE_R[0:1, :], SROW[0:1, 12:16], start=True, stop=True))
            em.c("dve", ["PSC_sw"], ["SWB"], lambda e: e.tensor_copy(SWB[:, :], PSC[:, 0:4]))

            for t, (kind, i, hf, srcap, dstap) in enumerate(halves):
                sl = t % 2
                si = 2 if kind == "u" else 3
                dmain("Dsp", W2[sl][:, :], srcap, f"W2_{sl}")
                em.c("pool", [f"W2_{sl}", "SWB"], ["QTx", "QTB"],
                     lambda e, sl=sl, si=si: e.tensor_scalar_mul(QT[:, :], W2[sl][:, :], SWB[:, si : si + 1]))
                em.c("dve", ["QTx"], ["QTy", "QTB"],
                     lambda e: e.tensor_scalar_add(QT[:, :], QT[:, :], MAGIC))
                em.c("dve", ["QTy"], ["QTz", "QTB"],
                     lambda e: e.tensor_scalar(QT[:, :], QT[:, :], -MAGIC, 1.0, ALU.add, ALU.min))
                em.c("dve", ["QTz", "QTB"], ["QOx"],
                     lambda e: e.tensor_scalar_max(QOUT[:, :], QT[:, :], -1.0))
                em.dma("Dsp", ["QOx"], [f"agq{t}"],
                       lambda e, dstap=dstap: e.dma_start(out=dstap, in_=QOUT[:, :]))
            if QMODE in (3, 4):
                em.cc([f"agq{t}" for t in range(8)], ["agu_out"],
                      lambda e: e.collective_compute("AllGather", ALU.bypass,
                                                     replica_groups=[list(range(NCORES))],
                                                     ins=[agu_in.ap().opt()], outs=[agu_out.ap().opt()]))
            if QMODE in (3, 5):
                em.cc([f"agq{t}" for t in range(8)], ["agd_out"],
                      lambda e: e.collective_compute("AllGather", ALU.bypass,
                                                     replica_groups=[list(range(NCORES))],
                                                     ins=[agd_in.ap().opt()], outs=[agd_out.ap().opt()]))
            if QMODE < 3:
                em.c("dve", [f"agq{t}" for t in range(8)], ["agu_out", "agd_out"],
                     lambda e: e.engine_nop())
        em.barrier()

        def bcast_row(row_name, row_ap, ncols, ps, psname, dst_name, dst_ap):
            for o in range(0, ncols, 512):
                w = min(512, ncols - o)
                em.c("pe", [row_name, "ONE_R"], [psname],
                     lambda e, o=o, w=w: e.matmul(ps[:, o : o + w], ONE_R[0:1, :],
                                                  row_ap[0:1, o : o + w], start=True, stop=True))
            em.c("dve", [psname], [dst_name],
                 lambda e: e.tensor_copy(dst_ap[:, 0:ncols], ps[:, 0:ncols]))

        def emit_ln(tag, XN, xn_pref):
            with contextlib.ExitStack() as ls:
                lsb = lambda n, s, d: ls.enter_context(nc.sbuf_tensor(n, s, d))
                X2 = [lsb(f"{tag}X2_{i}", [128, 512], F32) for i in range(2)]
                ABC = lsb(f"{tag}ABC", [128, 1024], F32)
                TT = [lsb(f"{tag}TT{i}", [128, 512], F32) for i in range(2)]
                R = ROWS
                for h in range(2):
                    for kc in range(DK):
                        xa = XRES[:, kc * TOK + 512 * h : kc * TOK + 512 * (h + 1)]
                        sl = kc % 2
                        em.c("act", [f"XRES{kc}"], [f"{tag}x2_{sl}"],
                             lambda e, xa=xa, sl=sl: e.activation(X2[sl][:, :], xa, AF.Square))
                        em.c("pe", [f"XRES{kc}", "ONE_F"], [f"PSR0_{tag}{h}"],
                             lambda e, xa=xa, kc=kc: e.matmul(PSR0[:, :], ONE_F[:, :], xa,
                                                              start=(kc == 0), stop=(kc == DK - 1)))
                        em.c("pe", [f"{tag}x2_{sl}", "ONE_F"], [f"PSR1_{tag}{h}"],
                             lambda e, sl=sl, kc=kc: e.matmul(PSR1[:, :], ONE_F[:, :], X2[sl][:, :],
                                                              start=(kc == 0), stop=(kc == DK - 1)))
                    em.c("dve", [f"PSR0_{tag}{h}", "ROWSBUF"], [f"{tag}m{h}", "ROWSBUF"],
                         lambda e: e.tensor_scalar_mul(R[0:1, 0:512], PSR0[0:1, :], float(1.0 / D)))
                    em.c("dve", [f"PSR1_{tag}{h}", "ROWSBUF"], [f"{tag}q{h}", "ROWSBUF"],
                         lambda e: e.tensor_scalar_mul(R[0:1, 512:1024], PSR1[0:1, :], float(1.0 / D)))
                    em.c("dve", [f"{tag}m{h}"], [f"{tag}mm{h}", "ROWSBUF"],
                         lambda e: e.tensor_mul(R[0:1, 1024:1536], R[0:1, 0:512], R[0:1, 0:512]))
                    em.c("dve", [f"{tag}q{h}", f"{tag}mm{h}"], [f"{tag}v{h}", "ROWSBUF"],
                         lambda e: e.tensor_sub(R[0:1, 512:1024], R[0:1, 512:1024], R[0:1, 1024:1536]))
                    em.c("act", [f"{tag}v{h}", "CPOOL"], [f"{tag}l{h}", "ROWSBUF"],
                         lambda e: e.activation(R[0:1, 512:1024], R[0:1, 512:1024], AF.Ln, bias=LN_EPS))
                    em.c("act", [f"{tag}l{h}", "CPOOL"], [f"{tag}r{h}", "ROWSBUF"],
                         lambda e: e.activation(R[0:1, 512:1024], R[0:1, 512:1024], AF.Exp, scale=-0.5))
                    em.c("dve", [f"{tag}r{h}", f"{tag}m{h}"], [f"{tag}b{h}", "ROWSBUF"],
                         lambda e: e.scalar_tensor_tensor(R[0:1, 1024:1536], R[0:1, 0:512], -1.0,
                                                          R[0:1, 512:1024], ALU.mult, ALU.mult))
                    em.c("pe", [f"{tag}r{h}", "ONE_R"], [f"PSC_{tag}a{h}"],
                         lambda e: e.matmul(PSC[:, 0:512], ONE_R[0:1, :], R[0:1, 512:1024],
                                            start=True, stop=True))
                    em.c("dve", [f"PSC_{tag}a{h}"], [f"{tag}abca{h}"],
                         lambda e: e.tensor_copy(ABC[:, 0:512], PSC[:, 0:512]))
                    em.c("pe", [f"{tag}b{h}", "ONE_R"], [f"PSC_{tag}b{h}"],
                         lambda e: e.matmul(PSC[:, 512:1024], ONE_R[0:1, :], R[0:1, 1024:1536],
                                            start=True, stop=True))
                    em.c("dve", [f"PSC_{tag}b{h}"], [f"{tag}abcb{h}"],
                         lambda e: e.tensor_copy(ABC[:, 512:1024], PSC[:, 512:1024]))
                    for kc in range(DK):
                        xa = XRES[:, kc * TOK + 512 * h : kc * TOK + 512 * (h + 1)]
                        xna = XN[:, kc * TOK + 512 * h : kc * TOK + 512 * (h + 1)]
                        sl = kc % 2
                        em.c("pool", [f"XRES{kc}", f"{tag}abca{h}"], [f"{tag}tt{sl}"],
                             lambda e, xa=xa, sl=sl: e.tensor_mul(TT[sl][:, :], xa, ABC[:, 0:512]))
                        em.c("dve", [f"{tag}tt{sl}", f"{tag}abcb{h}"], [f"{xn_pref}{kc}"],
                             lambda e, xna=xna, sl=sl: e.tensor_add(xna, TT[sl][:, :], ABC[:, 512:1024]))

        def col_to_row(cname, col_ap, rname, row_ap):
            em.c("pe", [cname, "IDF"], ["PSC_" + rname],
                 lambda e: e.transpose(PSC[0:1, 0:128], col_ap, IDF[:, :]))
            em.c("dve", ["PSC_" + rname], [rname],
                 lambda e: e.tensor_copy(row_ap, PSC[0:1, 0:128]))

        def emit_attn_a(XN, xnp):
            with contextlib.ExitStack() as s:
                ssb = lambda n, sh, d: s.enter_context(nc.sbuf_tensor(n, sh, d))
                Q1 = ssb("Q1", [128, TOK], BF16)
                F1 = ssb("F1", [128, TOK], BF16)
                with contextlib.ExitStack() as ws:
                    WQ1 = ws.enter_context(nc.sbuf_tensor("WQ1", [128, D], BF16))
                    WK1 = ws.enter_context(nc.sbuf_tensor("WK1", [128, D], BF16))
                    dmain("Dpool", WQ1[:, :], wq1_i[:, :], "WQ1")
                    dmain("Dpool", WK1[:, :], wk1_i[:, :], "WK1")
                    for h in range(2):
                        cols = slice(512 * h, 512 * (h + 1))
                        for kc in range(DK):
                            xna = XN[:, kc * TOK + 512 * h : kc * TOK + 512 * (h + 1)]
                            em.c("pe", [f"{xnp}{kc}", "WQ1"], ["PSA_q1"],
                                 lambda e, xna=xna, kc=kc, cols=cols: e.matmul(
                                     PSA[:, cols], WQ1[:, kc * 128 : (kc + 1) * 128], xna,
                                     start=(kc == 0), stop=(kc == DK - 1)))
                            em.c("pe", [f"{xnp}{kc}", "WK1"], ["PSB_f1"],
                                 lambda e, xna=xna, kc=kc, cols=cols: e.matmul(
                                     PSB[:, cols], WK1[:, kc * 128 : (kc + 1) * 128], xna,
                                     start=(kc == 0), stop=(kc == DK - 1)))
                    em.c("act", ["PSA_q1"], ["Q1"], lambda e: e.activation(Q1[:, :], PSA[:, :], AF.Identity))
                    em.c("act", ["PSB_f1"], ["F1"], lambda e: e.activation(F1[:, :], PSB[:, :], AF.Identity))
                em.barrier()

                CAR = ssb("CAR", [128, 64], F32)
                AGT = ssb("AGT", [128, NCORES * 64], F32)
                SCARRY = ssb("SCARRY", [128, 32], F32)
                AW = [ssb(f"AW{i}", [128, 1024], BF16) for i in range(2)]
                AW2 = [ssb(f"AW2_{i}", [128, 256], BF16) for i in range(2)]
                FT = [ssb("FT0", [128, TOK], F32)]
                ATL = ssb("ATL", [128, TOK], F32)
                HT = [ssb("HT0", [128, TOK], F32)]
                QC = ssb("QC", [128, TOK], BF16)
                RBC = ssb("RBC", [128, TOK], F32)
                WOB = [ssb(f"WOB{i}", [128, D], BF16) for i in range(2)]
                APR = ssb("APR", [128, 32], F32)
                HPR = ssb("HPR", [128, 32], F32)

                for c in range(EC):
                    sl = c % 2
                    ps2, ps2n = (PSB, "PSB") if sl == 0 else (PSA, "PSA")
                    dmain("Dpool", AW2[sl][:, :], aw2_i[:, 256 * c : 256 * (c + 1)], f"AW2_{sl}")
                    for h in range(2):
                        cols = slice(512 * h, 512 * (h + 1))
                        em.c("pe", ["F1", f"AW2_{sl}"], [f"PSC_f{c}"],
                             lambda e, cols=cols, sl=sl: e.matmul(
                                 PSC[:, cols], AW2[sl][:, 128:256], F1[:, cols], start=True, stop=True))
                    em.c("act", [f"PSC_f{c}"], ["ft"],
                         lambda e: e.activation(FT[0][:, :], PSC[:, :], AF.Sigmoid))
                    em.c("act", ["ft"], ["atl"],
                         lambda e, sl=sl: e.activation(ATL[:, :], FT[0][:, :], AF.Sigmoid))
                    for g in range(2):
                        aws = (2 * c + g) % 2
                        dmain("Dpool", AW[aws][:, :],
                              aw1_i[:, 2048 * c + 1024 * g : 2048 * c + 1024 * (g + 1)], f"AW{aws}")
                        for h in range(2):
                            cols = slice(512 * h, 512 * (h + 1))
                            for kk in range(8):
                                kc = 8 * g + kk
                                xna = XN[:, kc * TOK + 512 * h : kc * TOK + 512 * (h + 1)]
                                em.c("pe", [f"{xnp}{kc}", f"AW{aws}"], [f"{ps2n}_v{c}"],
                                     lambda e, ps2=ps2, cols=cols, aws=aws, kk=kk, kc=kc, xna=xna: e.matmul(
                                         ps2[:, cols], AW[aws][:, kk * 128 : (kk + 1) * 128], xna,
                                         start=(kc == 0), stop=(kc == DK - 1)))
                    em.c("dve", ["atl", f"{ps2n}_v{c}"], [f"SAb{c}"],
                         lambda e, ps2=ps2, c=c: e.scalar_tensor_tensor(
                             SA[:, TOK * c : TOK * (c + 1)], ATL[:, :], 1.0, ps2[:, :],
                             ALU.subtract, ALU.mult))
                    em.c("dve", ["atl", f"SAb{c}"], ["ht"],
                         lambda e, sl=sl, c=c: e.tensor_tensor_scan(
                             HT[0][:, :], ATL[:, :], SA[:, TOK * c : TOK * (c + 1)],
                             0.0, ALU.mult, ALU.add))
                    em.c("dve", ["ht"], [f"carh{c}"],
                         lambda e, sl=sl, c=c: e.tensor_copy(CAR[:, c : c + 1], HT[0][:, TOK - 1 : TOK]))
                    em.c("dve", ["atl"], [f"cara{c}"],
                         lambda e, c=c: e.tensor_reduce(CAR[:, 32 + c : 33 + c], ATL[:, :], AX, ALU.mult))

                em.dma("Dsp", [f"carh{c}" for c in range(EC)] + [f"cara{c}" for c in range(EC)],
                       ["car_in"], lambda e: e.dma_start(out=car_in.ap()[:, :], in_=CAR[:, :]))
                em.cc(["car_in"], ["car_out"],
                      lambda e: e.collective_compute("AllGather", ALU.bypass,
                                                     replica_groups=[list(range(NCORES))],
                                                     ins=[car_in.ap().opt()], outs=[car_out.ap().opt()]))
                dmain("Dsp", AGT[:, :].rearrange("p (r f) -> p r f", r=NCORES),
                      car_out.ap().rearrange("(r p) f -> p r f", p=128), "AGT", reads=["car_out"])
                memset(SCARRY[:, :], 0.0, "SCARRY")
                for r in range(NCORES):
                    em.c("dve", ["AGT", "CSEL"], ["APR"],
                         lambda e, r=r: e.tensor_scalar_mul(APR[:, :], AGT[:, 64 * r + 32 : 64 * r + 64],
                                                            CSEL[:, 8 + r : 9 + r]))
                    em.c("dve", ["APR", "CSEL"], ["APR"],
                         lambda e, r=r: e.tensor_scalar(APR[:, :], APR[:, :], 1.0,
                                                        CSEL[:, 8 + r : 9 + r], ALU.add, ALU.subtract))
                    em.c("dve", ["AGT", "CSEL"], ["HPR"],
                         lambda e, r=r: e.tensor_scalar_mul(HPR[:, :], AGT[:, 64 * r : 64 * r + 32],
                                                            CSEL[:, 8 + r : 9 + r]))
                    em.c("pool", ["SCARRY", "APR"], ["SCARRY"],
                         lambda e: e.tensor_mul(SCARRY[:, :], SCARRY[:, :], APR[:, :]))
                    em.c("dve", ["SCARRY", "HPR"], ["SCARRY"],
                         lambda e: e.tensor_add(SCARRY[:, :], SCARRY[:, :], HPR[:, :]))

                if dbg_out is not None:
                    em.dma("Dsp", [f"carh{c}" for c in range(EC)] + [f"cara{c}" for c in range(EC)],
                           ["dbg1"], lambda e: e.dma_start(out=dbg_out[:, 0:64], in_=CAR[:, :]))
                    em.dma("Dsp", ["SCARRY"], ["dbg2"],
                           lambda e: e.dma_start(out=dbg_out[:, 64:96], in_=SCARRY[:, :]))
                    em.dma("Dsp", ["atl"], ["dbg3"],
                           lambda e: e.dma_start(out=dbg_out[:, 96 : 96 + TOK], in_=ATL[:, :]))
                    em.dma("Dsp", ["ht"], ["dbg4"],
                           lambda e: e.dma_start(out=dbg_out[:, 96 + TOK : 96 + 2 * TOK], in_=HT[0][:, :]))
                    em.c("dve", ["SAb31", "RBC"], ["dbg5c", "RBC"],
                         lambda e: e.tensor_copy(RBC[:, :], SA[:, TOK * 31 : TOK * 32]))
                    em.dma("Dsp", ["dbg5c"], ["dbg5"],
                           lambda e: e.dma_start(out=dbg_out[:, 96 + 2 * TOK : 96 + 3 * TOK], in_=RBC[:, :]))
                for c in range(EC):
                    sl = c % 2
                    ps2, ps2n = (PSB, "PSB") if sl == 0 else (PSA, "PSA")
                    dmain("Dpool", AW2[sl][:, :], aw2_i[:, 256 * c : 256 * (c + 1)], f"AW2_{sl}")
                    for h in range(2):
                        cols = slice(512 * h, 512 * (h + 1))
                        em.c("pe", ["F1", f"AW2_{sl}"], [f"PSC_f2{c}"],
                             lambda e, cols=cols, sl=sl: e.matmul(
                                 PSC[:, cols], AW2[sl][:, 128:256], F1[:, cols], start=True, stop=True))
                        em.c("pe", ["Q1", f"AW2_{sl}"], [f"{ps2n}_q{c}"],
                             lambda e, ps2=ps2, cols=cols, sl=sl: e.matmul(
                                 ps2[:, cols], AW2[sl][:, 0:128], Q1[:, cols], start=True, stop=True))
                    em.c("act", [f"PSC_f2{c}"], ["ft"],
                         lambda e: e.activation(FT[0][:, :], PSC[:, :], AF.Sigmoid))
                    em.c("act", ["ft"], ["atl"],
                         lambda e, sl=sl: e.activation(ATL[:, :], FT[0][:, :], AF.Sigmoid))
                    em.c("act", [f"{ps2n}_q{c}"], ["qc"],
                         lambda e, ps2=ps2: e.activation(QC[:, :], ps2[:, :], AF.Silu))
                    em.c("dve", ["atl", f"SAb{c}", "SCARRY"], ["ht"],
                         lambda e, sl=sl, c=c: e.tensor_tensor_scan(
                             HT[0][:, :], ATL[:, :], SA[:, TOK * c : TOK * (c + 1)],
                             SCARRY[:, c : c + 1], ALU.mult, ALU.add))
                    em.c("dve", ["ht", "qc"], ["ht"],
                         lambda e, sl=sl: e.scalar_tensor_tensor(HT[0][:, :], QC[:, :], -1.0,
                                                                 HT[0][:, :], ALU.mult, ALU.mult))
                    em.c("act", ["ht"], ["ft"],
                         lambda e, sl=sl: e.activation(FT[0][:, :], HT[0][:, :], AF.Square))
                    for h in range(2):
                        cols = slice(512 * h, 512 * (h + 1))
                        em.c("pe", ["ft", "ONE_F"], [f"PSR{h}_g2"],
                             lambda e, cols=cols, c=c, h=h, sl=sl: e.matmul(
                                 (PSR0 if h == 0 else PSR1)[:, :], ONE_F[:, :], FT[0][:, cols],
                                 start=(c == 0), stop=(c == EC - 1)))
                    em.c("dve", ["ht"], [f"SAg{c}"],
                         lambda e, sl=sl, c=c: e.tensor_copy(SA[:, TOK * c : TOK * (c + 1)], HT[0][:, :]))

                for h in range(2):
                    em.c("act", [f"PSR{h}_g2", "CPOOL", "ROWSBUF"], [f"rmsl{h}", "ROWSBUF"],
                         lambda e, h=h: e.activation(ROWS[0:1, 512 * h : 512 * (h + 1)],
                                                     (PSR0 if h == 0 else PSR1)[0:1, :],
                                                     AF.Ln, bias=RMS_EPS, scale=float(1.0 / E2)))
                    em.c("act", [f"rmsl{h}", "CPOOL"], [f"rmsr{h}", "ROWSBUF"],
                         lambda e, h=h: e.activation(ROWS[0:1, 512 * h : 512 * (h + 1)],
                                                     ROWS[0:1, 512 * h : 512 * (h + 1)],
                                                     AF.Exp, scale=-0.5))
                    em.c("pe", [f"rmsr{h}", "ONE_R"], [f"PSC_r{h}"],
                         lambda e, h=h: e.matmul(PSC[:, 512 * h : 512 * (h + 1)], ONE_R[0:1, :],
                                                 ROWS[0:1, 512 * h : 512 * (h + 1)],
                                                 start=True, stop=True))
                    em.c("dve", [f"PSC_r{h}"], ["RBC"],
                         lambda e, h=h: e.tensor_copy(RBC[:, 512 * h : 512 * (h + 1)],
                                                      PSC[:, 512 * h : 512 * (h + 1)]))
                for c in range(EC):
                    em.c("dve", [f"SAg{c}", "RBC", "NW"], [f"SAn{c}"],
                         lambda e, c=c: e.scalar_tensor_tensor(
                             SA[:, TOK * c : TOK * (c + 1)], SA[:, TOK * c : TOK * (c + 1)],
                             NW[:, c : c + 1], RBC[:, :], ALU.mult, ALU.mult))
                for m in range(DK):
                    for g in range(2):
                        sl = (2 * m + g) % 2
                        ps, psn = (PSA, "PSA") if m % 2 == 0 else (PSB, "PSB")
                        dmain("Dpool", WOB[sl][:, :],
                              wo_i[:, E2 * m + 2048 * g : E2 * m + 2048 * (g + 1)], f"WOB{sl}")
                        for h in range(2):
                            cols = slice(512 * h, 512 * (h + 1))
                            for kk in range(DK):
                                k = g * DK + kk
                                em.c("pe", [f"SAn{k}", f"WOB{sl}"], [f"{psn}_o{m}"],
                                     lambda e, ps=ps, sl=sl, k=k, kk=kk, h=h, cols=cols, g=g: e.matmul(
                                         ps[:, cols], WOB[sl][:, kk * 128 : (kk + 1) * 128],
                                         SA[:, TOK * k + 512 * h : TOK * k + 512 * (h + 1)],
                                         start=(k == 0), stop=(k == EC - 1)))
                    em.c("dve", [f"{psn}_o{m}", f"XRES{m}"], [f"XRES{m}"],
                         lambda e, ps=ps, m=m: e.tensor_add(XRES[:, TOK * m : TOK * (m + 1)],
                                                            XRES[:, TOK * m : TOK * (m + 1)], ps[:, :]))

        def emit_ffn(XN, xnp, tag):
            with contextlib.ExitStack() as s:
                ssb = lambda n, sh, d: s.enter_context(nc.sbuf_tensor(n, sh, d))
                STRIP = ssb(f"{tag}STRIP", [128, 4096], F32)
                STMP = ssb(f"{tag}STMP", [128, 256], F32)
                MXT = ssb(f"{tag}MXT", [128, 256], F32)
                MC = ssb(f"{tag}MC", [128, 8], F32)
                RC = ssb(f"{tag}RC", [128, 128], F32)
                RQ = ssb(f"{tag}RQ", [1, 256], F32)
                FBC = ssb(f"{tag}FBC", [128, 256], F32)
                DDB = ssb(f"{tag}DDB", [128, TOK], F32)
                HEVA = ssb(f"{tag}HEVA", [128, TOK], F32)
                WUP = [ssb(f"{tag}WUP{i}", [128, 1024], BF16) for i in range(2)]
                WDN = [ssb(f"{tag}WDN{i}", [128, 1024], BF16) for i in range(2)]

                def fwht_quant(nchunks, srcap, srcn, dstap, dstn, strip, alpha, sw_col):
                    nblk = nchunks // 8
                    nstrips = TOK // strip
                    nbl = strip // 128
                    for st in range(nstrips):
                        t0 = st * strip
                        for ch in range(nchunks):
                            blk, o = divmod(ch, 8)
                            sl = ch % 2
                            ps, psn = (PSA, "PSA") if sl == 0 else (PSB, "PSB")
                            for k in range(8):
                                sgn = bin(o & k).count("1") % 2
                                hp = HPM[:, 0:128] if sgn == 0 else HPM[:, 128:256]
                                em.c("pe", [srcn(blk * 8 + k), "HPM"], [f"{psn}_{tag}A{ch}"],
                                     lambda e, ps=ps, hp=hp, k=k, src=srcap(blk * 8 + k, t0, strip),
                                            strip=strip: e.matmul(ps[:, 0:strip], hp, src,
                                                                  start=(k == 0), stop=(k == 7)))
                            em.c("act", [f"{psn}_{tag}A{ch}"], [f"{tag}S{ch}"],
                                 lambda e, ps=ps, ch=ch, strip=strip: e.activation(
                                     STRIP[:, strip * ch : strip * (ch + 1)], ps[:, 0:strip], AF.Identity))
                        step = 1
                        while step < nblk:
                            for o in range(8):
                                for base in range(0, nblk, 2 * step):
                                    for j in range(step):
                                        i0 = (base + j) * 8 + o
                                        i1 = (base + j + step) * 8 + o
                                        a0 = STRIP[:, strip * i0 : strip * (i0 + 1)]
                                        a1 = STRIP[:, strip * i1 : strip * (i1 + 1)]
                                        em.c("act", [f"{tag}S{i0}"], [f"{tag}stmp"],
                                             lambda e, a0=a0, strip=strip: e.activation(
                                                 STMP[:, 0:strip], a0, AF.Identity))
                                        em.c("dve", [f"{tag}S{i0}", f"{tag}S{i1}"], [f"{tag}S{i0}"],
                                             lambda e, a0=a0, a1=a1: e.tensor_add(a0, a0, a1))
                                        em.c("pool", [f"{tag}stmp", f"{tag}S{i1}"], [f"{tag}S{i1}"],
                                             lambda e, a1=a1, strip=strip: e.tensor_sub(a1, STMP[:, 0:strip], a1))
                            step *= 2
                        em.c("dve", [f"{tag}S{ch}" for ch in range(nchunks)], [f"{tag}mx"],
                             lambda e, strip=strip, nchunks=nchunks: e.reduce_max(
                                 MXT[:, 0:strip].rearrange("p (t o) -> p t o", o=1),
                                 STRIP[:, 0 : strip * nchunks].rearrange("p (c t) -> p t c", c=nchunks),
                                 AX, apply_absolute_value=True))
                        for b in range(nbl):
                            t = t0 + 128 * b
                            em.c("pe", [f"{tag}mx", "IDF"], [f"PSC_{tag}t"],
                                 lambda e, b=b: e.transpose(PSC[:, 0:128], MXT[:, 128 * b : 128 * (b + 1)],
                                                            IDF[:, :]))
                            em.c("dve", [f"PSC_{tag}t"], [f"{tag}rc"],
                                 lambda e: e.tensor_copy(RC[:, :], PSC[:, 0:128]))
                            em.c("dve", [f"{tag}rc"], [f"{tag}mcol"],
                                 lambda e: e.reduce_max(MC[:, 0:1], RC[:, :], AX))
                            em.c("dve", [f"{tag}mcol"], [f"{tag}mc2"],
                                 lambda e: e.tensor_scalar(MC[:, 1:2], MC[:, 0:1], float(alpha),
                                                           float(Q_EPS), ALU.mult, ALU.max))
                            em.c("dve", [f"{tag}mc2"], [f"{tag}mc3"],
                                 lambda e: e.reciprocal(MC[:, 2:3], MC[:, 1:2]))
                            em.c("dve", [f"{tag}mc3"], [f"{tag}fcol"],
                                 lambda e: e.tensor_scalar_mul(MC[:, 3:4], MC[:, 2:3], float(alpha * 127.0)))
                            em.c("dve", [f"{tag}mc2", "SWB"], [f"{tag}dcol"],
                                 lambda e, sw_col=sw_col: e.scalar_tensor_tensor(
                                     MC[:, 4:5], MC[:, 1:2], float(1.0 / 127.0),
                                     SWB[:, sw_col : sw_col + 1], ALU.mult, ALU.mult))
                            col_to_row(f"{tag}fcol", MC[:, 3:4], f"{tag}frow", RQ[0:1, 0:128])
                            col_to_row(f"{tag}dcol", MC[:, 4:5], f"{tag}drow", RQ[0:1, 128:256])
                            em.c("pe", [f"{tag}frow", "ONE_R"], [f"PSC_{tag}f"],
                                 lambda e: e.matmul(PSC[:, 0:128], ONE_R[0:1, :], RQ[0:1, 0:128],
                                                    start=True, stop=True))
                            em.c("dve", [f"PSC_{tag}f"], [f"{tag}fbc"],
                                 lambda e, b=b: e.tensor_copy(FBC[:, 128 * b : 128 * (b + 1)], PSC[:, 0:128]))
                            em.c("pe", [f"{tag}drow", "ONE_R"], [f"PSC_{tag}d"],
                                 lambda e: e.matmul(PSC[:, 128:256], ONE_R[0:1, :], RQ[0:1, 128:256],
                                                    start=True, stop=True))
                            em.c("dve", [f"PSC_{tag}d"], [f"{tag}ddb"],
                                 lambda e, t=t: e.tensor_copy(DDB[:, t : t + 128], PSC[:, 128:256]))
                        for ch in range(nchunks):
                            em.c("pool", [f"{tag}S{ch}", f"{tag}fbc"], [f"{tag}S{ch}"],
                                 lambda e, ch=ch, strip=strip: e.tensor_mul(
                                     STRIP[:, strip * ch : strip * (ch + 1)],
                                     STRIP[:, strip * ch : strip * (ch + 1)], FBC[:, 0:strip]))
                            em.c("dve", [f"{tag}S{ch}"], [f"{tag}S{ch}"],
                                 lambda e, ch=ch, strip=strip: e.tensor_scalar_add(
                                     STRIP[:, strip * ch : strip * (ch + 1)],
                                     STRIP[:, strip * ch : strip * (ch + 1)], MAGIC))
                            em.c("dve", [f"{tag}S{ch}"], [dstn(ch)],
                                 lambda e, d=dstap(ch, t0, strip), ch=ch, strip=strip: e.tensor_scalar_add(
                                     d, STRIP[:, strip * ch : strip * (ch + 1)], -MAGIC))

                fwht_quant(16,
                           lambda ch, t0, w: XN[:, ch * TOK + t0 : ch * TOK + t0 + w],
                           lambda ch: f"{xnp}{ch}",
                           lambda ch, t0, w: XN[:, ch * TOK + t0 : ch * TOK + t0 + w],
                           lambda ch: f"{xnp}{ch}",
                           256, ALPHA1, 0)
                for m in range(EC):
                    ps, psn = (PSA, "PSA") if m % 2 == 0 else (PSB, "PSB")
                    for g in range(2):
                        sl = (2 * m + g) % 2
                        em.dma("Dsp", ["agu_out"], [f"{tag}wu{sl}"],
                               lambda e, m=m, sl=sl, g=g: e.dma_start(
                                   out=WUP[sl][:, :].rearrange("p (kc m) -> p kc m", kc=8),
                                   in_=agu_out.ap().rearrange("(kc p) f -> p kc f", p=128)[
                                       :, 8 * g : 8 * (g + 1), 128 * m : 128 * (m + 1)]))
                        for h in range(2):
                            cols = slice(512 * h, 512 * (h + 1))
                            for kk in range(8):
                                kc = 8 * g + kk
                                em.c("pe", [f"{xnp}{kc}", f"{tag}wu{sl}"], [f"{psn}_u{m}"],
                                     lambda e, ps=ps, sl=sl, kc=kc, kk=kk, h=h, cols=cols: e.matmul(
                                         ps[:, cols], WUP[sl][:, kk * 128 : (kk + 1) * 128],
                                         XN[:, kc * TOK + 512 * h : kc * TOK + 512 * (h + 1)],
                                         start=(kc == 0), stop=(kc == DK - 1)))
                    em.c("dve", [f"{psn}_u{m}", f"{tag}ddb"], [f"{tag}he{m}", f"{tag}HV"],
                         lambda e, ps=ps: e.tensor_mul(HEVA[:, :], ps[:, :], DDB[:, :]))
                    em.c("act", [f"{tag}he{m}"], [f"{tag}hr{m}", f"{tag}HV"],
                         lambda e: e.activation(HEVA[:, :], HEVA[:, :], AF.Relu))
                    em.c("act", [f"{tag}hr{m}", f"{tag}HV"], [f"{tag}sa{m}"],
                         lambda e, m=m: e.activation(SA[:, TOK * m : TOK * (m + 1)], HEVA[:, :], AF.Square))
                fwht_quant(32,
                           lambda ch, t0, w: SA[:, ch * TOK + t0 : ch * TOK + t0 + w],
                           lambda ch: f"{tag}sa{ch}",
                           lambda ch, t0, w: SA[:, ch * TOK + t0 : ch * TOK + t0 + w],
                           lambda ch: f"{tag}sa{ch}",
                           128, ALPHA2, 1)
                for m in range(DK):
                    ps, psn = (PSA, "PSA") if m % 2 == 0 else (PSB, "PSB")
                    for g in range(4):
                        sl = (4 * m + g) % 2
                        em.dma("Dsp", ["agd_out"], [f"{tag}wd{sl}"],
                               lambda e, m=m, sl=sl, g=g: e.dma_start(
                                   out=WDN[sl][:, :].rearrange("p (kc m) -> p kc m", kc=8),
                                   in_=agd_out.ap().rearrange("(kc p) f -> p kc f", p=128)[
                                       :, 8 * g : 8 * (g + 1), 128 * m : 128 * (m + 1)]))
                        for h in range(2):
                            cols = slice(512 * h, 512 * (h + 1))
                            for kk in range(8):
                                kc = g * 8 + kk
                                em.c("pe", [f"{tag}sa{kc}", f"{tag}wd{sl}"], [f"{psn}_d{m}"],
                                     lambda e, ps=ps, sl=sl, kc=kc, kk=kk, h=h, cols=cols: e.matmul(
                                         ps[:, cols], WDN[sl][:, kk * 128 : (kk + 1) * 128],
                                         SA[:, TOK * kc + 512 * h : TOK * kc + 512 * (h + 1)],
                                         start=(kc == 0), stop=(kc == EC - 1)))
                    em.c("dve", [f"{psn}_d{m}", f"{tag}ddb"], [f"{tag}de{m}", f"{tag}HV"],
                         lambda e, ps=ps: e.tensor_mul(HEVA[:, :], ps[:, :], DDB[:, :]))
                    em.c("dve", [f"{tag}de{m}", f"{tag}HV", f"XRES{m}"], [f"XRES{m}"],
                         lambda e, m=m: e.tensor_add(XRES[:, TOK * m : TOK * (m + 1)],
                                                     XRES[:, TOK * m : TOK * (m + 1)], HEVA[:, :]))

        def emit_attn_b(XN, xnp):
            QOFF, KOFF, VOFF, HOFF, FOFF = 0, 16 * TOK, 20 * TOK, 24 * TOK, 28 * TOK
            with contextlib.ExitStack() as s:
                ssb = lambda n, sh, d: s.enter_context(nc.sbuf_tensor(n, sh, d))
                BW = [ssb(f"BW{i}", [128, D], BF16) for i in range(2)]
                VF = ssb("VF", [128, TOK], F32)
                RS1 = ssb("RS1", [128, TOK], BF16)
                RS2 = ssb("RS2", [128, TOK], BF16)
                CMB = [ssb(f"CMB{i}", [128, 2048], BF16) for i in range(2)]
                PR = [ssb(f"PR{i}", [128, TOK], BF16) for i in range(2)]
                BCT = ssb("BCT", [128, TOK], BF16)
                ROPE = ssb("ROPE", [128, 2 * TOK], BF16)
                FMASK = ssb("FMASK", [128, 2 * TOK], BF16)
                dmain("Dpool", ROPE[:, :], rope_i[:, :], "ROPE")
                dmain("Dsp", FMASK[:, :].rearrange("p (c t) -> p c t", c=2),
                      fmask_i.rearrange("(c p) t -> p c t", p=128), "FMASK")

                def proj(wimg, nheads, wname_pref, dst_off, evac):
                    for hh in range(nheads):
                        sl = hh % 2
                        ps, psn = (PSA, "PSA") if sl == 0 else (PSB, "PSB")
                        em.dma("Dpool", [], [f"{wname_pref}{sl}", f"BWS{sl}"],
                               lambda e, hh=hh, sl=sl: e.dma_start(out=BW[sl][:, :],
                                                                   in_=wimg[:, D * hh : D * (hh + 1)]))
                        for h in range(2):
                            cols = slice(512 * h, 512 * (h + 1))
                            for kc in range(DK):
                                em.c("pe", [f"{xnp}{kc}", f"{wname_pref}{sl}", f"BWS{sl}"],
                                     [f"{psn}_{wname_pref}{hh}"],
                                     lambda e, ps=ps, sl=sl, kc=kc, h=h, cols=cols: e.matmul(
                                         ps[:, cols], BW[sl][:, kc * 128 : (kc + 1) * 128],
                                         XN[:, kc * TOK + 512 * h : kc * TOK + 512 * (h + 1)],
                                         start=(kc == 0), stop=(kc == DK - 1)))
                        evac(hh, ps, f"{psn}_{wname_pref}{hh}")

                def evac_q(hh, ps, psn):
                    em.c("act", [psn], [f"saq{hh}"],
                         lambda e, ps=ps, hh=hh: e.activation(SA[:, QOFF + TOK * hh : QOFF + TOK * (hh + 1)],
                                                              ps[:, :], AF.Identity))

                def evac_k(hh, ps, psn):
                    em.c("act", [psn], [f"sak{hh}"],
                         lambda e, ps=ps, hh=hh: e.activation(SA[:, KOFF + TOK * hh : KOFF + TOK * (hh + 1)],
                                                              ps[:, :], AF.Identity))

                def evac_v(hh, ps, psn):
                    em.c("act", [psn], ["VF"],
                         lambda e, ps=ps: e.activation(VF[:, :], ps[:, :], AF.Identity))
                    for c in range(8):
                        em.c("pe", ["VF", "IDF"], ["PSC_vt"],
                             lambda e, c=c: e.transpose(PSC[:, 0:128], VF[:, 128 * c : 128 * (c + 1)], IDF[:, :]))
                        em.c("dve", ["PSC_vt"], [f"savt{hh}_{c}"],
                             lambda e, hh=hh, c=c: e.tensor_copy(
                                 SA[:, VOFF + (hh * 8 + c) * 128 : VOFF + (hh * 8 + c + 1) * 128], PSC[:, 0:128]))

                proj(bwq_i, NH, "bq", QOFF, evac_q)
                if dbg_out is not None and os.environ.get("KDBG") == "2":
                    em.dma("Dpool", ["saq0"], ["dbqpre_o"],
                           lambda e: e.dma_start(out=dbg_out[:, 96 + 3 * TOK : 96 + 4 * TOK],
                                                 in_=SA[:, QOFF : QOFF + TOK]))
                proj(bwk_i, NKV, "bk", KOFF, evac_k)
                proj(bwv_i, NKV, "bv", VOFF, evac_v)

                def rope(tile_name_r, tile_ap):
                    for h in range(2):
                        em.c("pe", [tile_name_r, "PSW"], ["PSC_sw"],
                             lambda e, h=h: e.matmul(PSC[:, 512 * h : 512 * (h + 1)], PSW[:, :],
                                                     tile_ap[:, 512 * h : 512 * (h + 1)],
                                                     start=True, stop=True))
                    em.c("dve", [tile_name_r, "ROPE"], ["RS1"],
                         lambda e: e.tensor_mul(RS1[:, :], tile_ap, ROPE[:, 0:TOK]))
                    em.c("dve", ["PSC_sw", "ROPE"], ["RS2"],
                         lambda e: e.tensor_mul(RS2[:, :], PSC[:, :], ROPE[:, TOK : 2 * TOK]))
                    em.c("dve", ["RS1", "RS2"], [tile_name_r],
                         lambda e: e.tensor_add(tile_ap, RS1[:, :], RS2[:, :]))

                for hh in range(NH):
                    rope(f"saq{hh}", SA[:, QOFF + TOK * hh : QOFF + TOK * (hh + 1)])
                for hh in range(NKV):
                    rope(f"sak{hh}", SA[:, KOFF + TOK * hh : KOFF + TOK * (hh + 1)])
                if dbg_out is not None and os.environ.get("KDBG") == "2":
                    em.dma("Dpool", ["saq0"], ["dbq_o"],
                           lambda e: e.dma_start(out=dbg_out[:, 96 : 96 + TOK], in_=SA[:, QOFF : QOFF + TOK]))
                    em.dma("Dpool", ["sak0"], ["dbk_o"],
                           lambda e: e.dma_start(out=dbg_out[:, 96 + TOK : 96 + 2 * TOK], in_=SA[:, KOFF : KOFF + TOK]))
                    em.dma("Dpool", [f"savt0_{c}" for c in range(8)], ["dbv_o"],
                           lambda e: e.dma_start(out=dbg_out[:, 96 + 2 * TOK : 96 + 3 * TOK], in_=SA[:, VOFF : VOFF + TOK]))
                for kvh in range(NKV):
                    em.dma("Dsp", [f"sak{kvh}"], [f"kvi_k{kvh}"],
                           lambda e, kvh=kvh: e.dma_start(out=kv_in.ap()[:, 512 * kvh : 512 * (kvh + 1)],
                                                          in_=SA[:, KOFF + TOK * kvh + 512 : KOFF + TOK * (kvh + 1)]))
                    em.dma("Dsp", [f"sak{kvh}"], [f"kvi_kf{kvh}"],
                           lambda e, kvh=kvh: e.dma_start(out=kv_in.ap()[:, 2048 + 256 * kvh : 2048 + 256 * (kvh + 1)],
                                                          in_=SA[:, KOFF + TOK * kvh : KOFF + TOK * kvh + 256]))
                    em.dma("Dsp", [f"savt{kvh}_{c}" for c in range(4, 8)], [f"kvi_v{kvh}"],
                           lambda e, kvh=kvh: e.dma_start(
                               out=kv_in.ap()[:, 3072 + 512 * kvh : 3072 + 512 * (kvh + 1)],
                               in_=SA[:, VOFF + (kvh * 8 + 4) * 128 : VOFF + (kvh * 8 + 8) * 128]))
                    em.dma("Dsp", [f"savt{kvh}_{c}" for c in range(2)], [f"kvi_vf{kvh}"],
                           lambda e, kvh=kvh: e.dma_start(
                               out=kv_in.ap()[:, 5120 + 256 * kvh : 5120 + 256 * (kvh + 1)],
                               in_=SA[:, VOFF + (kvh * 8) * 128 : VOFF + (kvh * 8 + 2) * 128]))
                em.cc([f"kvi_k{k}" for k in range(NKV)] + [f"kvi_kf{k}" for k in range(NKV)]
                      + [f"kvi_v{k}" for k in range(NKV)] + [f"kvi_vf{k}" for k in range(NKV)],
                      ["kv_out"],
                      lambda e: e.collective_compute("AllGather", ALU.bypass,
                                                     replica_groups=[list(range(NCORES))],
                                                     ins=[kv_in.ap().opt()], outs=[kv_out.ap().opt()]))
                for part, (c0, cw) in enumerate([(0, 2048), (3072, 2048)]):
                    hoff = HOFF + part * 2048
                    for r in range(NCORES):
                        sl = r % 2
                        em.dma("Dsp", ["kv_out"], [f"cmb{sl}"],
                               lambda e, r=r, sl=sl, c0=c0, cw=cw: e.dma_start(
                                   out=CMB[sl][:, 0:cw], in_=kv_out.ap()[128 * r : 128 * (r + 1), c0 : c0 + cw]))
                        if r == 0:
                            em.c("dve", [f"cmb{sl}", "CSEL"], [f"halo{part}"],
                                 lambda e, sl=sl, hoff=hoff, cw=cw: e.tensor_scalar_mul(
                                     SA[:, hoff : hoff + cw], CMB[sl][:, 0:cw], CSEL[:, 0:1]))
                        else:
                            em.c("dve", [f"cmb{sl}", "CSEL", f"halo{part}"], [f"halo{part}"],
                                 lambda e, sl=sl, r=r, hoff=hoff, cw=cw: e.scalar_tensor_tensor(
                                     SA[:, hoff : hoff + cw], CMB[sl][:, 0:cw], CSEL[:, r : r + 1],
                                     SA[:, hoff : hoff + cw], ALU.mult, ALU.add))
                em.dma("Dsp", ["kv_out"], ["front_k"],
                       lambda e: e.dma_start(out=SA[:, FOFF : FOFF + 1024], in_=kv_out.ap()[0:128, 2048:3072]))
                em.dma("Dsp", ["kv_out"], ["front_v"],
                       lambda e: e.dma_start(out=SA[:, FOFF + 1024 : FOFF + 2048], in_=kv_out.ap()[0:128, 5120:6144]))

                for hh in range(NH):
                    kvh = hh // 4
                    qname = f"saq{hh}"
                    qap = lambda q0, w: SA[:, QOFF + TOK * hh + q0 : QOFF + TOK * hh + q0 + w]
                    srcs = []
                    for j in range(4):
                        srcs.append(("halo", j,
                                     SA[:, HOFF + kvh * 512 + 128 * j : HOFF + kvh * 512 + 128 * (j + 1)],
                                     SA[:, HOFF + 2048 + (kvh * 4 + j) * 128 : HOFF + 2048 + (kvh * 4 + j + 1) * 128],
                                     0, (j + 1) * 128, [f"halo0", f"halo1"]))
                    for c in range(8):
                        q0 = c * 128
                        qw = (min(c + 4, 7) - c + 1) * 128
                        srcs.append(("local", c,
                                     SA[:, KOFF + TOK * kvh + 128 * c : KOFF + TOK * kvh + 128 * (c + 1)],
                                     SA[:, VOFF + (kvh * 8 + c) * 128 : VOFF + (kvh * 8 + c + 1) * 128],
                                     q0, qw, [f"sak{kvh}", f"savt{kvh}_{c}"]))
                    for fc in range(2):
                        srcs.append(("front", fc,
                                     SA[:, FOFF + kvh * 256 + 128 * fc : FOFF + kvh * 256 + 128 * (fc + 1)],
                                     SA[:, FOFF + 1024 + (kvh * 2 + fc) * 128 : FOFF + 1024 + (kvh * 2 + fc + 1) * 128],
                                     0, TOK, ["front_k", "front_v"]))

                    av_started_banks = set()
                    dn_started_banks = set()
                    nsrc = len(srcs)
                    for si, (kind, idx, kap, vap, q0, qw, knames) in enumerate(srcs):
                        sps, spn = (PSA, "PSA") if si % 2 == 0 else (PSC, "PSC")
                        prn = si % 2
                        for o in range(0, qw, 512):
                            w = min(512, qw - o)
                            em.c("pe", [qname] + knames, [f"{spn}_sc{hh}_{si}"],
                                 lambda e, sps=sps, kap=kap, o=o, w=w, rhs=qap(q0 + o, w): e.matmul(
                                     sps[:, o : o + w], kap, rhs, start=True, stop=True))
                        em.c("act", [f"{spn}_sc{hh}_{si}", "CPOOL"], [f"pr{prn}"],
                             lambda e, sps=sps, prn=prn, qw=qw: e.activation(
                                 PR[prn][:, 0:qw], sps[:, 0:qw], AF.Exp, bias=SM_SHIFT, scale=SM_SCALE))
                        if kind == "local":
                            em.c("dve", [f"pr{prn}", "TRIM"], [f"pr{prn}"],
                                 lambda e, prn=prn: e.tensor_mul(PR[prn][:, 0:128], PR[prn][:, 0:128],
                                                                 TRIM[:, 0:128]))
                            if idx <= 3:
                                em.c("dve", [f"pr{prn}", "TRIM"], [f"pr{prn}"],
                                     lambda e, prn=prn, qw=qw: e.tensor_mul(PR[prn][:, qw - 128 : qw],
                                                                            PR[prn][:, qw - 128 : qw],
                                                                            TRIM[:, 128:256]))
                        elif kind == "halo":
                            em.c("dve", [f"pr{prn}", "CSEL"], [f"pr{prn}"],
                                 lambda e, prn=prn, qw=qw: e.tensor_scalar_mul(PR[prn][:, 0:qw],
                                                                               PR[prn][:, 0:qw], CSEL[:, 16:17]))
                            em.c("dve", [f"pr{prn}", "TRIM"], [f"pr{prn}"],
                                 lambda e, prn=prn, qw=qw: e.tensor_mul(PR[prn][:, qw - 128 : qw],
                                                                        PR[prn][:, qw - 128 : qw],
                                                                        TRIM[:, 128:256]))
                        else:
                            em.c("dve", [f"pr{prn}", "FMASK"], [f"pr{prn}"],
                                 lambda e, prn=prn, idx=idx: e.tensor_mul(PR[prn][:, 0:TOK], PR[prn][:, 0:TOK],
                                                                          FMASK[:, TOK * idx : TOK * (idx + 1)]))
                        a = q0
                        while a < q0 + qw:
                            b_end = min(q0 + qw, (a // 512 + 1) * 512)
                            w = b_end - a
                            bank = a // 512
                            o = a - q0
                            psr = PSR0 if bank == 0 else PSR1
                            dst = a % 512
                            st = bank not in dn_started_banks
                            dn_started_banks.add(bank)
                            em.c("pe", [f"pr{prn}", "ONE_B"], [f"PSR{bank}_dn{hh}"],
                                 lambda e, psr=psr, dst=dst, w=w, prn=prn, o=o, st=st: e.matmul(
                                     psr[0:1, dst : dst + w], ONE_B[:, :], PR[prn][:, o : o + w],
                                     start=st, stop=(si == nsrc - 1)))
                            stb = bank not in av_started_banks
                            av_started_banks.add(bank)
                            em.c("pe", [f"pr{prn}"] + knames, [f"PSB_av{hh}_{bank}"],
                                 lambda e, vap=vap, prn=prn, o=o, w=w, a=a, stb=stb: e.matmul(
                                     PSB[:, a : a + w], vap, PR[prn][:, o : o + w],
                                     start=stb, stop=False))
                            a = b_end
                    em.c("dve", [f"PSR0_dn{hh}", f"PSR1_dn{hh}", "ROWSBUF"], [f"drw{hh}", "ROWSBUF"],
                         lambda e: (e.tensor_copy(ROWS[0:1, 0:512], PSR0[0:1, :]),
                                    e.tensor_copy(ROWS[0:1, 512:1024], PSR1[0:1, :]))[1])
                    em.c("dve", [f"drw{hh}"], [f"drr{hh}", "ROWSBUF"],
                         lambda e: e.reciprocal(ROWS[0:1, 0:1024], ROWS[0:1, 0:1024]))
                    for o in range(0, TOK, 512):
                        em.c("pe", [f"drr{hh}", "ONE_R"], [f"PSC_nb{hh}"],
                             lambda e, o=o: e.matmul(PSC[:, o : o + 512], ONE_R[0:1, :],
                                                     ROWS[0:1, o : o + 512], start=True, stop=True))
                    em.c("dve", [f"PSC_nb{hh}"], [f"bct{hh}"],
                         lambda e: e.tensor_copy(BCT[:, :], PSC[:, :]))
                    em.c("dve", [f"bct{hh}", f"PSB_av{hh}_0", f"PSB_av{hh}_1"], [f"{xnp}{hh}"],
                         lambda e, hh=hh: e.tensor_mul(XN[:, TOK * hh : TOK * (hh + 1)], PSB[:, :], BCT[:, :]))

                if dbg_out is not None and os.environ.get("KDBG") == "2":
                    em.dma("Dpool", [f"{xnp}0"], ["dbs_o"],
                           lambda e: e.dma_start(out=dbg_out[:, 0:TOK] if False else dbg_out[:, 96 + 2 * TOK : 96 + 3 * TOK],
                                                 in_=XN[:, 0:TOK]))
                for m in range(DK):
                    sl = m % 2
                    ps, psn = (PSA, "PSA") if sl == 0 else (PSB, "PSB")
                    em.dma("Dpool", [], [f"bo{sl}", f"BWS{sl}"],
                           lambda e, m=m, sl=sl: e.dma_start(out=BW[sl][:, :], in_=bwo_i[:, D * m : D * (m + 1)]))
                    for h in range(2):
                        cols = slice(512 * h, 512 * (h + 1))
                        for k in range(NH):
                            em.c("pe", [f"{xnp}{k}", f"bo{sl}", f"BWS{sl}"], [f"{psn}_bo{m}"],
                                 lambda e, ps=ps, sl=sl, k=k, h=h, cols=cols: e.matmul(
                                     ps[:, cols], BW[sl][:, k * 128 : (k + 1) * 128],
                                     XN[:, TOK * k + 512 * h : TOK * k + 512 * (h + 1)],
                                     start=(k == 0), stop=(k == NH - 1)))
                    em.c("dve", [f"{psn}_bo{m}", f"XRES{m}"], [f"XRES{m}"],
                         lambda e, ps=ps, m=m: e.tensor_add(XRES[:, TOK * m : TOK * (m + 1)],
                                                            XRES[:, TOK * m : TOK * (m + 1)], ps[:, :]))

        sub = 0
        if sub < n_sub:
            with contextlib.ExitStack() as xs:
                XN1 = xs.enter_context(nc.sbuf_tensor("XN1", [128, DK * TOK], BF16))
                emit_ln("L1", XN1, "xn1_")
                emit_attn_a(XN1, "xn1_")
            em.barrier()
        sub += 1
        if sub < n_sub:
            with contextlib.ExitStack() as xs:
                XN2 = xs.enter_context(nc.sbuf_tensor("XN2", [128, DK * TOK], BF16))
                emit_ln("L2", XN2, "xn2_")
                emit_ffn(XN2, "xn2_", "f1")
            em.barrier()
        sub += 1
        if sub < n_sub:
            with contextlib.ExitStack() as xs:
                XN3 = xs.enter_context(nc.sbuf_tensor("XN3", [128, DK * TOK], BF16))
                emit_ln("L3", XN3, "xn3_")
                emit_attn_b(XN3, "xn3_")
            em.barrier()
        sub += 1
        if sub < n_sub:
            with contextlib.ExitStack() as xs:
                XN4 = xs.enter_context(nc.sbuf_tensor("XN4", [128, DK * TOK], BF16))
                emit_ln("L4", XN4, "xn4_")
                emit_ffn(XN4, "xn4_", "f2")
            em.barrier()

        em.dma("Dsp", [f"XRES{kc}" for kc in range(DK)], ["y_out"],
               lambda e: e.dma_start(out=y_out.rearrange("(c p) t -> p c t", p=128), in_=XRES[:, :].rearrange("p (c t) -> p c t", c=DK)))
        em.barrier()
        em.c("sp", [], [], lambda e: e.nop())

        em.finalize()

        sems = {}
        with contextlib.ExitStack() as semctx:
            for name in ["pe", "act", "dve", "pool", "sp", "cc", "Dsp", "Dact", "Dpool"]:
                sems[name] = semctx.enter_context(nc.semaphore(f"sem_{name}"))
            with nc.Block() as block:
                @block.tensor
                def _(e):
                    em.replay("pe", e, sems)

                @block.scalar
                def _(e):
                    em.replay("act", e, sems)

                @block.vector
                def _(e):
                    em.replay("dve", e, sems)

                @block.gpsimd
                def _(e):
                    em.replay("pool", e, sems)

                @block.sync
                def _(e):
                    em.replay("sp", e, sems)

    return nc


def _prep_shared(inp):
    W = {}
    W["wq1"] = pack_kc(np.asarray(inp["L0_attn_a_Wq1"], np.float32))
    W["wk1"] = pack_kc(np.asarray(inp["L0_attn_a_Wk1"], np.float32))
    wq2 = np.asarray(inp["L0_attn_a_Wq2"], np.float32)
    wk2 = np.asarray(inp["L0_attn_a_Wk2"], np.float32)
    wv = np.asarray(inp["L0_attn_a_Wv"], np.float32)
    aw1 = np.empty((128, EC * 2048), np.float32)
    aw2 = np.empty((128, EC * 256), np.float32)
    for c in range(EC):
        aw1[:, 2048 * c : 2048 * (c + 1)] = pack_kc(wv[:, 128 * c : 128 * (c + 1)])
        aw2[:, 256 * c : 256 * c + 128] = wq2[:, 128 * c : 128 * (c + 1)]
        aw2[:, 256 * c + 128 : 256 * c + 256] = wk2[:, 128 * c : 128 * (c + 1)]
    W["aw1"], W["aw2"] = aw1, aw2
    W["wo_a"] = pack_mc(np.asarray(inp["L0_attn_a_Wo"], np.float32))
    W["nw_fm"] = np.ascontiguousarray(
        np.asarray(inp["L0_attn_a_norm_w"], np.float32).reshape(EC, 128).T)
    W["up_img"] = np.ascontiguousarray(np.asarray(inp["L0_ffn_w_up"], np.float32).T)
    W["dn_img"] = np.ascontiguousarray(np.asarray(inp["L0_ffn_w_down"], np.float32).T)

    perm = np.concatenate([np.concatenate([h * 128 + np.arange(0, 128, 2),
                                           h * 128 + np.arange(1, 128, 2)])
                           for h in range(NH)])
    permk = perm[: NKV * 128]
    bwq = np.asarray(inp["L0_attn_b_wq"], np.float32)[:, perm]
    bwk = np.asarray(inp["L0_attn_b_wk"], np.float32)[:, permk]
    bwv = np.asarray(inp["L0_attn_b_wv"], np.float32)
    W["bwq"] = np.concatenate([pack_kc(bwq[:, 128 * h : 128 * (h + 1)]) for h in range(NH)], 1)
    W["bwk"] = np.concatenate([pack_kc(bwk[:, 128 * h : 128 * (h + 1)]) for h in range(NKV)], 1)
    W["bwv"] = np.concatenate([pack_kc(bwv[:, 128 * h : 128 * (h + 1)]) for h in range(NKV)], 1)
    W["bwo"] = pack_mc(np.asarray(inp["L0_attn_b_wo"], np.float32))

    h128 = hadamard128()
    W["hpm"] = np.concatenate([h128, -h128], 1).astype(BF)
    jj = np.arange(128)[:, None]
    qq = np.arange(128)[None, :]
    W["trim"] = np.concatenate([(jj <= qq).astype(np.float32),
                                (jj > qq).astype(np.float32)], 1).astype(BF)
    psw = np.zeros((128, 128), np.float32)
    psw[np.arange(64), np.arange(64) + 64] = 1.0
    psw[np.arange(64) + 64, np.arange(64)] = 1.0
    W["psw"] = psw.astype(BF)
    W["idf"] = np.eye(128, dtype=np.float32)
    return W


def _prep_core(c):
    P = {}
    pos = np.arange(TOK * c, TOK * (c + 1))
    inv_freq = 1.0 / 10000.0 ** (np.arange(0, DQK_ := 128, 2)[:64].astype(np.float32) / 128)
    ang = np.outer(pos.astype(np.float32), inv_freq)
    cos = np.cos(ang).astype(np.float32).T
    sin = np.sin(ang).astype(np.float32).T
    c2 = np.concatenate([cos, cos], 0)
    s2 = np.concatenate([-sin, sin], 0)
    P["ropetab"] = np.ascontiguousarray(np.concatenate([c2, s2], 1))
    j = np.arange(256)[:, None]
    i = pos[None, :]
    fm = ((j < 256) & (j <= i - 512)).astype(np.float32)
    P["fmask"] = np.ascontiguousarray(fm).astype(BF)
    csel = np.zeros((128, 24), np.float32)
    if c > 0:
        csel[:, (c - 1) % NCORES] = 1.0
        csel[:, 16] = 1.0
    for r in range(NCORES):
        if r < c:
            csel[:, 8 + r] = 1.0
    P["csel"] = csel
    return P


_NC_CACHE = {}


def kernel(**inputs):
    n_sub = int(inputs.pop("_n_sub", 4))
    x = np.asarray(inputs["L0_attn_a_x"], np.float32)
    W = _prep_shared(inputs)
    if n_sub not in _NC_CACHE:
        _NC_CACHE[n_sub] = build_program(n_sub)
    nc = _NC_CACHE[n_sub]

    in_maps = []
    for c in range(NCORES):
        pc = _prep_core(c)
        m = {
            "x_fm": np.ascontiguousarray(x[0, TOK * c : TOK * (c + 1), :].T),
            "wq1": W["wq1"], "wk1": W["wk1"], "aw1": W["aw1"], "aw2": W["aw2"],
            "wo_a": W["wo_a"], "nw_fm": W["nw_fm"],
            "wq_up_sl": np.ascontiguousarray(W["up_img"][256 * c : 256 * (c + 1), :]),
            "wq_dn_sl": np.ascontiguousarray(W["dn_img"][512 * c : 512 * (c + 1), :]),
            "bwq": W["bwq"], "bwk": W["bwk"], "bwv": W["bwv"], "bwo": W["bwo"],
            "hpm": W["hpm"], "trim": W["trim"], "psw": W["psw"], "idf": W["idf"],
            "ropetab": pc["ropetab"], "fmask": pc["fmask"], "csel": pc["csel"],
        }
        in_maps.append(m)

    res = run_bass_kernel_spmd(nc, in_maps, core_ids=list(range(NCORES)))
    out = np.empty((1, NCORES * TOK, D), np.float32)
    for c in range(NCORES):
        out[0, TOK * c : TOK * (c + 1), :] = np.asarray(res.results[c]["y_fm"]).T
    if os.environ.get("KDBG"):
        kernel._dbg = [np.asarray(res.results[c].get("dbg")) for c in range(NCORES)]
    return out
